# version 20
# baseline (speedup 1.0000x reference)
import numpy as np
import ml_dtypes
import concourse.bass as bass
import concourse.mybir as mybir
from concourse.bass_utils import run_bass_kernel_spmd

F32 = mybir.dt.float32
BF16 = mybir.dt.bfloat16
AF = mybir.ActivationFunctionType
ALU = mybir.AluOpType
AX = mybir.AxisListType
BF = ml_dtypes.bfloat16

SEQ = 4096
H = 1024
NT = SEQ // 128
HC = H // 128
NE = 8
CAP = float(SEQ * 8 // NE)
EPS = 1e-5
ENGINES = ("sync", "gpsimd", "tensor", "vector", "scalar")


class Pl:

    ENG_SEMS = ("pe", "dve", "act")

    def __init__(self, nc):
        self.nc = nc
        self.q = {e: [] for e in ENGINES}
        self.sems = {n: nc.alloc_semaphore("s_" + n) for n in self.ENG_SEMS}
        self.cnt = {n: 0 for n in self.ENG_SEMS}
        self.lastw = {}
        self.g_lastoh = 0
        self.g_laste8 = 0

    def _stream(self, name):
        if name not in self.sems:
            self.sems[name] = self.nc.alloc_semaphore("s_" + name)
            self.cnt[name] = 0
        return self.sems[name]

    def dma(self, eng, fn, stream):
        s = self._stream(stream)
        self.cnt[stream] += 16
        self.q[eng].append(lambda h, fn=fn, s=s: fn(h).then_inc(s, 16))
        return self.cnt[stream]

    def cc(self, fn, stream):
        s = self._stream(stream)
        self.cnt[stream] += 1
        self.q["gpsimd"].append(lambda h, fn=fn, s=s: fn(h).then_inc(s, 1))
        return self.cnt[stream]

    def wait(self, eng, name, val=None):
        if name not in self.ENG_SEMS:
            self._stream(name)
            val = None
        v = self.cnt[name] if val is None else val
        if v <= 0:
            return
        key = (eng, name)
        if self.lastw.get(key, -1) >= v:
            return
        self.lastw[key] = v
        s = self.sems[name]
        self.q[eng].append(lambda h, s=s, v=v: h.wait_ge(s, v))

    def wait_many(self, eng, pairs):
        for nm, v in pairs:
            self.wait(eng, nm, v)

    def waitv(self, eng, name, v):
        if v <= 0:
            return
        key = (eng, name)
        if self.lastw.get(key, -1) >= v:
            return
        self.lastw[key] = v
        s = self._stream(name)
        self.q[eng].append(lambda h, s=s, v=v: h.wait_ge(s, v))

    AUTO = {"vector": "dve", "scalar": "act"}

    def do(self, eng, fn, inc=None):
        auto = self.AUTO.get(eng)
        if auto is not None:
            self.wait(eng, auto, self.cnt[auto])
            if inc is None:
                inc = auto
            assert inc == auto
        if inc is None:
            self.q[eng].append(fn)
            return None
        s = self.sems[inc]
        self.cnt[inc] += 1
        self.q[eng].append(lambda h, fn=fn, s=s: fn(h).then_inc(s, 1))
        return self.cnt[inc]

    def join(self, engs=ENGINES, skip=()):
        snap = dict(self.cnt)
        for e in engs:
            for n, v in snap.items():
                if n in skip:
                    continue
                self.wait(e, n, v)

    def play(self, block):
        for eng in ENGINES:
            ops = self.q[eng]
            if not ops:
                continue
            getattr(block, eng)(lambda h, ops=ops: [f(h) for f in ops])


class Prog:
    def __init__(self):
        self.nc = bass.Bass("TRN2", target_bir_lowering=False)
        nc = self.nc
        self.ps = nc.alloc_psum_tensor("ps", [128, 4096], F32)
        self.identb = nc.inline_tensor(np.eye(128, dtype=BF), "c_identb")
        self.identf = nc.inline_tensor(np.eye(128, dtype=np.float32), "c_identf")
        self.c_triu = nc.inline_tensor(
            np.triu(np.ones((128, 128), np.float32), 1), "c_triu")
        self.c_onescol = nc.inline_tensor(np.ones((128, 1), np.float32), "c_onescol")
        self.c_onesrow = nc.inline_tensor(np.ones((1, 128), np.float32), "c_onesrow")
        self.c_zero1 = nc.inline_tensor(np.zeros((128, 1), np.float32), "c_zero1")
        self.c_m1024 = nc.inline_tensor(
            np.full((128, 1), -1024.0, np.float32), "c_m1024")
        self.guards = []

    def sb(self, name, shape, dtype):
        g = self.nc.sbuf_tensor(name, shape, dtype)
        t = g.__enter__()
        self.guards.append(g)
        return t

    def free_to(self, mark):
        while len(self.guards) > mark:
            self.guards.pop().__exit__(None, None, None)

    def bankf(self, g, n=512):
        return self.ps[:, g * 512:g * 512 + n]

    def bankb(self, g):
        v = self.ps[:, g * 512:(g + 1) * 512].bitcast(BF16)
        assert tuple(v.shape) == (128, 1024), v.shape
        return v


def emit_ln(P, pl, xsrc, xn_dst, sq, st, wait_src, prev_act):
    pl.wait_many("vector", wait_src)
    pl.wait_many("scalar", wait_src)
    pl.wait("vector", "act", prev_act)
    pl.do("vector", lambda h: h.reduce_sum(st["mraw"][:], xsrc, axis=AX.X))
    pl.do("vector", lambda h: h.scalar_tensor_tensor(
        sq, xsrc, 1.0, xsrc, op0=ALU.mult, op1=ALU.mult, accum_out=st["sums"][:]))
    dm = pl.do("vector", lambda h: h.tensor_scalar(
        st["mean"][:], st["mraw"][:], 1.0 / 1024.0, 0.0, op0=ALU.mult, op1=ALU.add),
        inc="dve")
    pl.wait("vector", "dve", dm)
    pl.do("vector", lambda h: h.tensor_scalar(
        st["msq"][:], st["mean"][:], st["mean"][:, 0:1], -EPS,
        op0=ALU.mult, op1=ALU.add))
    d = pl.do("vector", lambda h: h.scalar_tensor_tensor(
        st["var"][:], st["sums"][:], 1.0 / 1024.0, st["msq"][:],
        op0=ALU.mult, op1=ALU.subtract), inc="dve")
    pl.wait("scalar", "dve", d)
    pl.do("scalar", lambda h: h.activation(st["lnv"][:], st["var"][:], AF.Ln,
                                           scale=1.0))
    a = pl.do("scalar", lambda h: h.activation(st["rstd"][:], st["lnv"][:], AF.Exp,
                                               scale=-0.5), inc="act")
    pl.wait("vector", "act", a)
    dnb = pl.do("vector", lambda h: h.tensor_scalar(
        st["nb2"][:], st["mean"][:], st["rstd"][:, 0:1], -1.0,
        op0=ALU.mult, op1=ALU.mult), inc="dve")
    pl.wait("scalar", "dve", dnb)
    d2 = pl.do("scalar", lambda h: h.activation(
        xn_dst, xsrc, AF.Identity, bias=st["nb2"][:, 0:1],
        scale=st["rstd"][:, 0:1]), inc="act")
    return ("act", d2), [("act", d2), ("dve", dnb)], d2


def emit_transpose8(P, pl, src_cols, bank, wait_src, free_at, identb_s):
    tgt = P.bankb(bank)
    pl.wait_many("tensor", [wait_src])
    pl.wait_many("tensor", free_at)
    pe = None
    for c in range(HC):
        pe = pl.do("tensor", lambda h, c=c: h.transpose(
            tgt[:, c * 128:(c + 1) * 128], src_cols(c), identb_s[:]),
            inc="pe" if c == HC - 1 else None)
    return pe


def emit_tr8_dma(P, pl, eng, src_col, dst_col, stream, waits):
    pl.wait_many(eng, waits)
    mk = None
    for c in range(HC):
        mk = pl.dma(eng, lambda h, c=c: h.dma_start(
            dst_col(c), src_col(c), transpose=True), stream)
    return mk


def emit_gating(P, pl, i, xnT_col, wgb, counts_only, gat, b1_free=()):
    lg = P.bankf(1, 512)[:, 0:NE]
    pl.wait("tensor", "dve", pl.g_lastoh)
    pl.wait("tensor", "act", pl.g_laste8)
    pl.wait_many("tensor", list(b1_free))
    pe = None
    for c in range(HC):
        pe = pl.do("tensor", lambda h, c=c: h.matmul(
            lg, xnT_col(c), wgb[:, c * NE:(c + 1) * NE],
            start=(c == 0), stop=(c == HC - 1)),
            inc="pe" if c == HC - 1 else None)
    pl.wait("vector", "pe", pe)
    d = pl.do("vector", lambda h: h.reduce_max(gat["mx8"][:], lg, axis=AX.X), inc="dve")
    pl.wait("vector", "dve", d)
    doh = pl.do("vector", lambda h: h.tensor_scalar(
        gat["oh"][:], lg, gat["mx8"][:, 0:1], 0.0, op0=ALU.subtract, op1=ALU.is_ge),
        inc="dve")
    pl.g_lastoh = doh
    if not counts_only:
        dn = pl.do("vector", lambda h: h.tensor_scalar(
            gat["nmx"][:], gat["mx8"][:], -1.0, 0.0, op0=ALU.mult, op1=ALU.add),
            inc="dve")
        pl.wait("scalar", "dve", dn)
        a = pl.do("scalar", lambda h: h.activation(
            gat["e8"][:], lg, AF.Exp, bias=gat["nmx"][:, 0:1], scale=1.0,
            accum_out=gat["z8"][:]), inc="act")
        pl.g_laste8 = a
        pl.wait("vector", "act", a)
        pl.do("vector", lambda h: h.reciprocal(gat["gate0"][:], gat["z8"][:]))
    pecnt = None
    if counts_only:
        pl.wait("tensor", "dve", doh)
        pecnt = pl.do("tensor", lambda h, i=i: h.matmul(
            P.bankf(3, 512)[0:1, 0:NE], gat["onescol"][:], gat["oh"][:],
            start=(i == 0), stop=(i == NT - 1)), inc="pe")
    return doh, pecnt


def emit_count_phase(P, pl, src_hbm, xbuf, sq, xn, stats, xnT_t, wgb, gat,
                     cnt_s, identb_s):
    xbuf_free = [[("dve", pl.cnt["dve"])]] * 2
    xn_free = [[("pe", pl.cnt["pe"])]] * 2
    xnTt_free = [[("pe", pl.cnt["pe"])]] * 2
    b0_free = [("dve", pl.cnt["dve"])]
    prev_act = [pl.cnt["act"], pl.cnt["act"]]
    for i in range(NT):
        par = i % 2
        pl.wait_many("sync", xbuf_free[par])
        sld = pl.dma("sync", lambda h, i=i, par=par: h.dma_start(
            out=xbuf[par][:], in_=src_hbm[i * 128:(i + 1) * 128, :]), "x%d" % par)
        xn_evt, xfree, pa = emit_ln(P, pl, xbuf[par][:], xn[par][:], sq[:], stats[par],
                                    [("x%d" % par, sld)] + xn_free[par], prev_act[par])
        prev_act[par] = pa
        xbuf_free[par] = list(xfree)
        pe_t = emit_transpose8(P, pl, lambda c, par=par: xn[par][:, c * 128:(c + 1) * 128],
                               0, xn_evt, b0_free + xnTt_free[par], identb_s)
        xn_free[par] = [("pe", pe_t)]
        pl.wait("vector", "pe", pe_t)
        dev = pl.do("vector", lambda h, par=par: h.tensor_copy(
            xnT_t[par][:], P.bankb(0)), inc="dve")
        b0_free = [("dve", dev)]
        pl.wait("tensor", "dve", dev)
        doh, pecnt = emit_gating(
            P, pl, i, lambda c, par=par: xnT_t[par][:, c * 128:(c + 1) * 128],
            wgb, True, gat)
        xnTt_free[par] = [("pe", pecnt)]
    pl.wait("vector", "pe", pl.cnt["pe"])
    dcv = pl.do("vector", lambda h: h.tensor_copy(
        cnt_s[:], P.bankf(3, 512)[0:1, 0:NE]), inc="dve")
    return dcv


def emit_collective(P, pl, myrowT_s, cnt_s, tbl_s, cc_in, cc_out, cnts_s):
    pe = pl.do("tensor", lambda h: h.matmul(
        P.bankf(2, 512)[0:NE, 0:NE], myrowT_s[:], cnt_s[:],
        start=True, stop=True), inc="pe")
    pl.wait("vector", "pe", pe)
    d = pl.do("vector", lambda h: h.tensor_copy(
        tbl_s[:], P.bankf(2, 512)[0:NE, 0:NE]), inc="dve")
    pl.wait("scalar", "dve", d)
    pl.dma("scalar", lambda h: h.dma_start(out=cc_in[:, :], in_=tbl_s[:]), "ccd")
    pl.wait("gpsimd", "ccd")
    pl.cc(lambda h: h.collective_compute(
        "AllReduce", ALU.add, replica_groups=[list(range(8))],
        ins=[cc_in.ap().opt()], outs=[cc_out.ap().opt()]), "ccc")
    pl.wait("gpsimd", "ccc")
    pl.dma("gpsimd", lambda h: h.dma_start(out=cnts_s[:], in_=cc_out[:, :]), "ccr")
    return d


def stage_weights(Wq, Wk, Wv, Wo, wg, W, b):
    def qk(WT):
        return np.ascontiguousarray(
            WT.reshape(HC, 128, 8, 128).transpose(1, 2, 0, 3).reshape(128, 8192))

    def vo(WT):
        return np.ascontiguousarray(
            WT.reshape(HC, 128, H).transpose(1, 0, 2).reshape(128, 8192))

    out = {}
    out["wq"] = qk(np.ascontiguousarray(Wq.T))
    out["wk"] = qk(np.ascontiguousarray(Wk.T))
    out["wv"] = vo(np.ascontiguousarray(Wv.T))
    out["wo"] = vo(np.ascontiguousarray(Wo.T))
    out["wg"] = np.ascontiguousarray(
        wg.reshape(HC, 128, NE).transpose(1, 0, 2).reshape(128, HC * NE))
    WT = W.transpose(0, 2, 1)
    out["wt"] = np.ascontiguousarray(
        WT.reshape(NE, HC, 128, H).transpose(2, 0, 1, 3).reshape(128, NE * HC * H))
    out["b"] = np.ascontiguousarray(b)
    return out


def emit_attention(P, pl, bufs, qsrc, ksrc, vsrc, xres_hbm, dst_hbm, scale,
                   oproj, identb_s, trail=None):
    xbuf = bufs["xbuf"]
    outb = bufs["outb"]
    Abuf, ATb = bufs["Abuf"], bufs["ATb"]
    zp, sm = bufs["zp"], bufs["sm"]
    cbias = bufs["cbias"]
    qload = bufs.get("qload")
    kvm = bufs.get("kv_marks")

    xbuf_free = [[("dve", pl.cnt["dve"])]] * 2
    outb_free = [[("so0", 0)], [("so1", 0)]]
    qbuf_free = [[("pe", pl.cnt["pe"])]] * 2
    bank_free = {g: [("dve", pl.cnt["dve"]), ("act", pl.cnt["act"])]
                 for g in range(8)}
    st8 = {}

    T = trail
    if T is not None:
        tr_xnfree = [[("xt", 0)], [("xt", 0)]]
        tr_prev = [pl.cnt["act"], pl.cnt["act"]]
        tr_mark = {}
        tr_cmm = {}
        tr_doh = {}
        tr_pc = {}
        tr_last = {"dC": 0, "B": None}
        pl.wait("tensor", "wgl")

    def trail_A(t2, ppar2, dxo2):
        par = t2 % 2
        xn_evt, xfree, pa = emit_ln(P, pl, outb[ppar2][:], T["xn"][par][:],
                                    T["sq"][:], T["stats"][par],
                                    [("dve", dxo2)] + tr_xnfree[par],
                                    tr_prev[par])
        tr_prev[par] = pa
        outb_free[ppar2].extend(xfree)
        war = [("pe", tr_cmm[t2 - 2])] if t2 - 2 in tr_cmm else []
        mk = emit_tr8_dma(P, pl, "sync",
                          lambda c, par=par: T["xn"][par][:, c * 128:(c + 1) * 128],
                          lambda c, par=par: T["xnT"][par][:, c * 128:(c + 1) * 128],
                          "xt", [xn_evt] + war)
        tr_mark[t2] = mk
        tr_xnfree[par] = [("xt", mk)]

    def trail_B(tB, at_ev0):
        pl.waitv("tensor", "xt", tr_mark[tB])
        pl.wait("tensor", "dve",
                at_ev0 if at_ev0 is not None else pl.cnt["dve"])
        if tr_last["dC"]:
            pl.wait("tensor", "dve", tr_last["dC"])
        par = tB % 2
        cm = None
        for c in range(HC):
            cm = pl.do("tensor", lambda h, c=c, par=par: h.matmul(
                P.bankf(0, 512)[:, 0:NE],
                T["xnT"][par][:, c * 128:(c + 1) * 128],
                T["wgb"][:, c * NE:(c + 1) * NE],
                start=(c == 0), stop=(c == HC - 1)),
                inc="pe" if c == HC - 1 else None)
        tr_cmm[tB] = cm
        pl.wait("vector", "pe", cm)
        pl.do("vector", lambda h: h.tensor_copy(
            T["lgs"][:], P.bankf(0, 512)[:, 0:NE]))
        d1 = pl.do("vector", lambda h: h.reduce_max(
            T["tmx"][:], T["lgs"][:], axis=AX.X), inc="dve")
        pl.wait("vector", "dve", d1)
        if tB - 2 in tr_pc:
            pl.wait("vector", "pe", tr_pc[tB - 2])
        doh = pl.do("vector", lambda h, par=par: h.tensor_scalar(
            T["ohb"][par][:], T["lgs"][:], T["tmx"][:, 0:1], 0.0,
            op0=ALU.subtract, op1=ALU.is_ge), inc="dve")
        tr_doh[tB] = doh
        tr_last["B"] = tB
        return [("dve", doh)]

    def trail_C(tC):
        pl.wait("tensor", "dve", tr_doh[tC])
        if tr_last["B"] is not None:
            pl.wait("tensor", "dve", tr_doh[tr_last["B"]])
        par = tC % 2
        pc = pl.do("tensor", lambda h, par=par: h.matmul(
            P.bankf(0, 512)[0:1, 16:16 + NE], T["onescol"][:], T["ohb"][par][:],
            start=True, stop=True), inc="pe")
        tr_pc[tC] = pc
        pl.wait("vector", "pe", pc)
        if tC == 0:
            dC = pl.do("vector", lambda h: h.tensor_copy(
                T["cnt_s"][:], P.bankf(0, 512)[0:1, 16:16 + NE]), inc="dve")
        else:
            dC = pl.do("vector", lambda h: h.tensor_add(
                T["cnt_s"][:], T["cnt_s"][:], P.bankf(0, 512)[0:1, 16:16 + NE]),
                inc="dve")
        tr_last["dC"] = dC
        return [("dve", dC)]

    for j in range(NT + 1):
        par = j % 2
        ppar = (j - 1) % 2
        t = st8.get(j - 1)
        if j < NT:
            if qload is not None:
                pl.wait_many("sync", qbuf_free[par])
                qstrm = qload(pl, j, par)
            else:
                qstrm = None
            pl.wait_many("sync", xbuf_free[par])
            pl.dma("sync", lambda h, j=j, par=par: h.dma_start(
                out=xbuf[par][:], in_=xres_hbm[j * 128:(j + 1) * 128, :]),
                "x%d" % par)
        if t is not None:
            pl.wait("vector", "act", t["exp"][7])
            pl.do("vector", lambda h: h.reduce_sum(sm["zs"][:], zp[:], axis=AX.X))
            drz = pl.do("vector", lambda h: h.reciprocal(sm["rz"][:], sm["zs"][:]),
                        inc="dve")
            pe_tr = {}
            at_ev = {}
            pe_cx = {}
            for bk in (4, 5):
                pl.wait_many("tensor", bank_free[bk])
            pl.wait("tensor", "act", t["exp"][5])
            if kvm is not None:
                pl.wait("tensor", "pkv")

            def emit_tr(tb):
                for kc in range(tb * 8, tb * 8 + 8):
                    pl.wait("tensor", "act", t["exp"][kc // 4])
                    pl.wait("tensor", "act", t["exp"][tb])
                    pe_tr[tb] = pl.do("tensor", lambda h, kc=kc, tb=tb: h.transpose(
                        P.bankb(tb)[:, (kc % 8) * 128:(kc % 8 + 1) * 128],
                        Abuf[:, kc * 128:(kc + 1) * 128], identb_s[:]),
                        inc="pe" if kc % 8 == 7 else None)
                pl.wait("vector", "pe", pe_tr[tb])
                at_ev[tb] = pl.do("vector", lambda h, tb=tb: h.tensor_copy(
                    ATb[:, tb * 1024:(tb + 1) * 1024], P.bankb(tb)), inc="dve")
                bank_free[tb] = [("dve", at_ev[tb])]

            emit_tr(0)
            for tb in range(4):
                if tb + 1 < 4:
                    emit_tr(tb + 1)
                pl.wait("tensor", "dve", at_ev[tb])
                for kc in range(tb * 8, tb * 8 + 8):
                    for half in range(2):
                        pe_cx[half] = pl.do("tensor", lambda h, kc=kc, half=half:
                                            h.matmul(
                            P.bankf(4 + half), ATb[:, kc * 128:(kc + 1) * 128],
                            vsrc(kc, half),
                            start=(kc == 0), stop=(kc == NT - 1)),
                            inc="pe" if kc == NT - 1 else None)
                if tb == 0 and T is not None:
                    extra0 = []
                    if j - 3 >= 0:
                        extra0 += trail_B(j - 3, at_ev[0])
                    if j - 4 >= 0:
                        extra0 += trail_C(j - 4)
                    bank_free[0] = bank_free[0] + extra0
            if oproj:
                ctxb, ctxT, wob = bufs["ctxb"], bufs["ctxT"], bufs["wob"]
                pl.wait("scalar", "dve", drz)
                cx_ev = None
                for half in range(2):
                    pl.wait("scalar", "pe", pe_cx[half])
                    cx_ev = pl.do("scalar", lambda h, half=half: h.activation(
                        ctxb[:, half * 512:(half + 1) * 512], P.bankf(4 + half),
                        AF.Identity, scale=sm["rz"][:, 0:1]), inc="act")
                    bank_free[4 + half] = [("act", cx_ev)]
                pe_ct = emit_transpose8(
                    P, pl, lambda c: ctxb[:, c * 128:(c + 1) * 128],
                    6, ("act", cx_ev), bank_free[6], identb_s)
                pl.wait("scalar", "pe", pe_ct)
                ct_ev = pl.do("scalar", lambda h: h.activation(
                    ctxT[:], P.bankb(6), AF.Identity), inc="act")
                bank_free[6] = [("act", ct_ev)]
            else:
                pl.wait_many("vector", [("x%d" % ppar, 0)])
                pl.wait_many("vector", outb_free[ppar])
                dxo = None
                for half in range(2):
                    pl.wait("vector", "pe", pe_cx[half])
                    dxo = pl.do("vector", lambda h, half=half, ppar=ppar:
                                h.scalar_tensor_tensor(
                        outb[ppar][:, half * 512:(half + 1) * 512],
                        P.bankf(4 + half), sm["rz"][:, 0:1],
                        xbuf[ppar][:, half * 512:(half + 1) * 512],
                        op0=ALU.mult, op1=ALU.add), inc="dve")
                    bank_free[4 + half] = [("dve", dxo)]
                xbuf_free[ppar] = [("dve", dxo)]
                pl.wait("gpsimd", "dve", dxo)
                gst = pl.dma("gpsimd", lambda h, j=j, ppar=ppar: h.dma_start(
                    out=dst_hbm[(j - 1) * 128:j * 128, :], in_=outb[ppar][:]),
                    "so%d" % ppar)
                outb_free[ppar] = [("so%d" % ppar, gst)]
                t["dxo"] = dxo
                t["ppar"] = ppar
        if j < NT:
            if qstrm is not None:
                pl.wait("tensor", qstrm)
            pe_s = {}
            for gb in range(6):
                pl.wait_many("tensor", bank_free[gb])
                if kvm is not None:
                    pl.waitv("tensor", "pk", kvm[gb])
                for c in range(HC):
                    pe_s[gb] = pl.do("tensor", lambda h, c=c, gb=gb, j=j, par=par:
                                     h.matmul(
                        P.bankf(gb), qsrc(c, j, par), ksrc(c, gb),
                        start=(c == 0), stop=(c == HC - 1)),
                        inc="pe" if c == HC - 1 else None)
        if t is not None and oproj:
            pl.wait("tensor", "act", ct_ev)
            pl.wait_many("tensor", bank_free[7])
            pe_o = {}
            for c in range(HC):
                for half in range(2):
                    pe_o[half] = pl.do("tensor", lambda h, c=c, half=half: h.matmul(
                        P.bankf(6 + half), bufs["ctxT"][:, c * 128:(c + 1) * 128],
                        bufs["wob"][:, c * 1024 + half * 512:
                                    c * 1024 + (half + 1) * 512],
                        start=(c == 0), stop=(c == HC - 1)),
                        inc="pe" if c == HC - 1 else None)
        if t is not None and oproj:
            pl.wait_many("vector", [("x%d" % ppar, 0)])
            pl.wait_many("vector", outb_free[ppar])
            dxo = None
            for half in range(2):
                pl.wait("vector", "pe", pe_o[half])
                dxo = pl.do("vector", lambda h, half=half, ppar=ppar:
                            h.tensor_add(
                    outb[ppar][:, half * 512:(half + 1) * 512],
                    P.bankf(6 + half),
                    xbuf[ppar][:, half * 512:(half + 1) * 512]), inc="dve")
                bank_free[6 + half] = [("dve", dxo)]
            xbuf_free[ppar] = [("dve", dxo)]
            pl.wait("gpsimd", "dve", dxo)
            gst = pl.dma("gpsimd", lambda h, j=j, ppar=ppar: h.dma_start(
                out=dst_hbm[(j - 1) * 128:j * 128, :], in_=outb[ppar][:]),
                "so%d" % ppar)
            outb_free[ppar] = [("so%d" % ppar, gst)]
            t["dxo"] = dxo
            t["ppar"] = ppar
        if j < NT:
            for gb in (6, 7):
                pl.wait_many("tensor", bank_free[gb])
                if kvm is not None:
                    pl.waitv("tensor", "pk", kvm[gb])
                for c in range(HC):
                    pe_s[gb] = pl.do("tensor", lambda h, c=c, gb=gb, j=j, par=par:
                                     h.matmul(
                        P.bankf(gb), qsrc(c, j, par), ksrc(c, gb),
                        start=(c == 0), stop=(c == HC - 1)),
                        inc="pe" if c == HC - 1 else None)
            qbuf_free[par] = [("pe", pe_s[7])]
            if t is not None:
                pl.wait("scalar", "pe", pe_tr[3])
            a_exp = {}
            for gb in range(8):
                pl.wait("scalar", "pe", pe_s[gb])
                a_exp[gb] = pl.do("scalar", lambda h, gb=gb: h.activation(
                    Abuf[:, gb * 512:(gb + 1) * 512], P.bankf(gb), AF.Exp,
                    bias=cbias[:, 0:1], scale=scale,
                    accum_out=zp[:, gb:gb + 1]), inc="act")
            st8[j] = {"exp": a_exp}
        if T is not None and t is not None:
            trail_A(j - 1, t["ppar"], t["dxo"])

    if T is not None:
        trail_B(NT - 2, None)
        trail_C(NT - 3)
        trail_B(NT - 1, None)
        trail_C(NT - 2)
        trail_C(NT - 1)


def emit_moe(P, pl, sh, mg, src_hbm, dst_hbm, wtb, stage=None):
    xbuf, sq, xn, stats = sh["xbuf"], sh["sq"], sh["xn"], sh["stats"]
    xnT_t, wgb, sel_s, cnts_s = sh["xnT_t"], sh["wgb"], sh["sel_s"], sh["cnts_s"]
    tri_s, identf_s = sh["tri_s"], sh["identf_s"]
    onesrow_s, onescol_s, identb_s = sh["onesrow_s"], sh["onescol_s"], sh["identb_s"]
    gat = dict(sh["gat"])
    gat.update(mg["gat_x"])
    acc, xob, gt_s, runbase, b_s = mg["acc"], mg["xob"], mg["gt_s"], \
        mg["runbase"], mg["b_s"]
    lg = P.bankf(1, 512)[:, 0:NE]

    xb4 = xbuf + mg["xb2"]
    xn4 = xn + mg["xn2"]
    st4 = stats + mg["st2"]
    xnT4 = xnT_t + mg["xnT2"]
    bank_free = {g: [("dve", pl.cnt["dve"])] for g in range(8)}
    xbuf_free = [[("dve", pl.cnt["dve"])]] * 4
    xn_free = [[("pe", pl.cnt["pe"])]] * 4
    xnTt_free = [[("pe", pl.cnt["pe"])]] * 4
    acc_free = [[("dve", pl.cnt["dve"])]] * 2
    xob_free = [[("so0", 0)], [("so1", 0)]]
    prev_act = [pl.cnt["act"]] * 4
    rb = {"pe": 0, "dve": 0}
    st_stage = {"prev": [pl.cnt["act"]] * 2,
                "xnfree": [[("pe", 0)], [("pe", 0)]],
                "tfree": [[("pe", 0)], [("pe", 0)]]}
    st_prep = {}
    st_ln = {}
    st_dev = {}
    st_gate = {}
    abias_of = {}

    def prep_ln(j):
        par = j % 4
        pl.wait_many("sync", xbuf_free[par])
        sld = pl.dma("sync", lambda h, j=j, par=par: h.dma_start(
            out=xb4[par][:], in_=src_hbm[j * 128:(j + 1) * 128, :]), "x%d" % par)
        xn_evt, xfree, pa = emit_ln(P, pl, xb4[par][:], xn4[par][:], sq[:],
                                    st4[par], [("x%d" % par, sld)] + xn_free[par],
                                    prev_act[par])
        prev_act[par] = pa
        xbuf_free[par] = list(xfree)
        st_ln[j] = xn_evt

    def prep_tr(j):
        par = j % 4
        pe_t = emit_transpose8(P, pl, lambda c, par=par: xn4[par][:, c * 128:(c + 1) * 128],
                               0, st_ln[j], bank_free[0] + xnTt_free[par],
                               identb_s)
        xn_free[par] = [("pe", pe_t)]
        pl.wait("vector", "pe", pe_t)
        dev = pl.do("vector", lambda h, par=par: h.tensor_copy(
            xnT4[par][:], P.bankb(0)), inc="dve")
        bank_free[0] = [("dve", dev)]
        st_dev[j] = dev

    def prep_lg(j):
        par = j % 4
        pl.wait("tensor", "dve", max(st_dev[j], pl.g_lastoh))
        pl.wait("tensor", "act", pl.g_laste8)
        pe_lg = None
        for c in range(HC):
            pe_lg = pl.do("tensor", lambda h, c=c, par=par: h.matmul(
                lg, xnT4[par][:, c * 128:(c + 1) * 128], wgb[:, c * NE:(c + 1) * NE],
                start=(c == 0), stop=(c == HC - 1)),
                inc="pe" if c == HC - 1 else None)
        st_prep[j] = pe_lg

    def gate(i):
        par = i % 2
        G = gat["G"][par]
        pl.wait("vector", "pe", st_prep[i])
        d = pl.do("vector", lambda h: h.reduce_max(gat["mx8"][:], lg, axis=AX.X),
                  inc="dve")
        pl.wait("vector", "dve", d)
        doh = pl.do("vector", lambda h: h.tensor_scalar(
            gat["oh"][:], lg, gat["mx8"][:, 0:1], 0.0, op0=ALU.subtract,
            op1=ALU.is_ge), inc="dve")
        pl.g_lastoh = doh
        dn = pl.do("vector", lambda h: h.tensor_scalar(
            gat["nmx"][:], gat["mx8"][:], -1.0, 0.0, op0=ALU.mult, op1=ALU.add),
            inc="dve")
        pl.wait("scalar", "dve", dn)
        a8 = pl.do("scalar", lambda h: h.activation(
            gat["e8"][:], lg, AF.Exp, bias=gat["nmx"][:, 0:1], scale=1.0,
            accum_out=gat["z8"][:]), inc="act")
        pl.g_laste8 = a8
        pl.wait("vector", "act", a8)
        pl.do("vector", lambda h: h.reciprocal(gat["gate0"][:], gat["z8"][:]))
        pl.wait_many("tensor", bank_free[2])
        pl.wait("tensor", "dve", max(doh, rb["dve"]))
        pl.do("tensor", lambda h: h.matmul(
            P.bankf(2, 512)[:, 0:NE], tri_s[:], gat["oh"][:], start=True, stop=False))
        peL = pl.do("tensor", lambda h: h.matmul(
            P.bankf(2, 512)[:, 0:NE], onesrow_s[:], runbase[:], start=False,
            stop=True), inc="pe")
        rb["pe"] = peL
        pl.wait_many("tensor", bank_free[3])
        pecs = pl.do("tensor", lambda h: h.matmul(
            P.bankf(3, 512)[0:1, 0:NE], onescol_s[:], gat["oh"][:],
            start=True, stop=True), inc="pe")
        pl.wait("vector", "pe", peL)
        pl.do("vector", lambda h: h.tensor_mul(
            gat["tmp8"][:], P.bankf(2, 512)[:, 0:NE], gat["oh"][:]))
        dls = pl.do("vector", lambda h: h.reduce_sum(
            gat["Ls"][:], gat["tmp8"][:], axis=AX.X), inc="dve")
        pl.do("vector", lambda h: h.tensor_scalar(
            gat["keep"][:], gat["Ls"][:], CAP, 1.0, op0=ALU.is_lt, op1=ALU.mult))
        dgg = pl.do("vector", lambda h: h.tensor_mul(
            gat["gg"][:], gat["keep"][:], gat["gate0"][:]), inc="dve")
        pl.wait("vector", "dve", dgg)
        dG = pl.do("vector", lambda h, G=G: h.tensor_scalar(
            G[:], gat["oh"][:], gat["gg"][:, 0:1], 0.0, op0=ALU.mult,
            op1=ALU.add), inc="dve")
        pl.wait("vector", "pe", max(pecs, rb["pe"]))
        rb["dve"] = pl.do("vector", lambda h: h.tensor_add(
            runbase[:], runbase[:], P.bankf(3, 512)[0:1, 0:NE]), inc="dve")
        st_gate[i] = (dG, dls, rb["dve"], par, G)

    def gate_b(i):
        dG, dls, rbdve, par, G = st_gate[i]
        pl.wait("tensor", "dve", max(dG, rbdve, dls))
        pegt = pl.do("tensor", lambda h, G=G: h.transpose(
            P.ps[0:8, 3 * 512:3 * 512 + 128], G[:], identf_s[:]), inc="pe")
        pl.wait("vector", "pe", pegt)
        dgt = pl.do("vector", lambda h: h.tensor_copy(
            gt_s[:], P.ps[0:8, 3 * 512:3 * 512 + 128]), inc="dve")
        pl.wait("tensor", "dve", dgt)
        pebias = {}
        for half in range(2):
            pebias[half] = pl.do("tensor", lambda h, half=half: h.matmul(
                P.bankf(2 + half), gt_s[:], b_s[0:NE, half * 512:(half + 1) * 512],
                start=True, stop=True), inc="pe")
        pl.wait_many("scalar", acc_free[par])
        abias = None
        for half in range(2):
            pl.wait("scalar", "pe", pebias[half])
            abias = pl.do("scalar", lambda h, half=half, par=par: h.activation(
                acc[par][:, half * 512:(half + 1) * 512], P.bankf(2 + half),
                AF.Identity), inc="act")
        bank_free[2] = [("act", abias), ("dve", dls)]
        bank_free[3] = [("act", abias)]
        abias_of[i] = abias

    prep_ln(0)
    prep_tr(0)
    prep_lg(0)
    prep_ln(1)
    prep_tr(1)
    pl.wait("tensor", "ccr")
    pl.wait("tensor", "c")
    pl.wait("tensor", "dve", pl.cnt["dve"])
    perb = pl.do("tensor", lambda h: h.matmul(
        P.bankf(2, 512)[0:1, 0:NE], sel_s[:], cnts_s[:], start=True, stop=True),
        inc="pe")
    pl.wait("vector", "pe", perb)
    drb = pl.do("vector", lambda h: h.tensor_copy(
        runbase[:], P.bankf(2, 512)[0:1, 0:NE]), inc="dve")
    rb["pe"] = perb
    rb["dve"] = drb
    gate(0)
    gate_b(0)
    prep_lg(1)
    for i in range(NT):
        par = i % 2
        par4 = i % 4
        ebanks = [(4, 5), (6, 7)]
        pe_e = {}
        dacc = None
        abias = abias_of.pop(i)
        for e in range(NE):
            ba, bb = ebanks[e % 2]
            pl.wait("tensor", "wt%d" % e)
            for bk in (ba, bb):
                pl.wait_many("tensor", bank_free[bk])
            for c in range(HC):
                for half, bk in ((0, ba), (1, bb)):
                    pe_e[(e, half)] = pl.do(
                        "tensor", lambda h, c=c, e=e, bk=bk, half=half, par4=par4:
                        h.matmul(
                            P.bankf(bk), xnT4[par4][:, c * 128:(c + 1) * 128],
                            wtb[:, (e * 8 + c) * 1024 + half * 512:
                                (e * 8 + c) * 1024 + (half + 1) * 512],
                            start=(c == 0), stop=(c == HC - 1)),
                        inc="pe" if c == HC - 1 else None)
            if e == 1 and i + 2 < NT:
                prep_ln(i + 2)
            if e == 3 and i + 1 < NT:
                if i + 2 < NT:
                    prep_tr(i + 2)
            if e == 4 and i + 1 < NT:
                gate(i + 1)
            if e == 5 and i + 2 < NT:
                prep_lg(i + 2)
            if e == 6 and i + 1 < NT:
                gate_b(i + 1)
            if e == 0:
                pl.wait("vector", "act", abias)
            for half, bk in ((0, ba), (1, bb)):
                pl.wait("vector", "pe", pe_e[(e, half)])
                dacc = pl.do("vector", lambda h, half=half, bk=bk, e=e, par=par:
                             h.scalar_tensor_tensor(
                    acc[par][:, half * 512:(half + 1) * 512], P.bankf(bk),
                    gat["G"][par][:, e:e + 1],
                    acc[par][:, half * 512:(half + 1) * 512],
                    op0=ALU.mult, op1=ALU.add), inc="dve")
                bank_free[bk] = [("dve", dacc)]
        xnTt_free[par4] = [("pe", pl.cnt["pe"])]
        pl.wait_many("vector", xob_free[par])
        dxo = pl.do("vector", lambda h, par=par, par4=par4: h.tensor_add(
            xob[par][:], acc[par][:], xb4[par4][:]), inc="dve")
        acc_free[par] = [("dve", dxo)]
        xbuf_free[par4] = xbuf_free[par4] + [("dve", dxo)]
        pl.wait("gpsimd", "dve", dxo)
        gst = pl.dma("gpsimd", lambda h, i=i, par=par: h.dma_start(
            out=dst_hbm[i * 128:(i + 1) * 128, :], in_=xob[par][:]), "so%d" % par)
        xob_free[par] = [("so%d" % par, gst)]


def alloc_moe_group(P, wt_name):
    mg = {}
    mg["wtb"] = P.sb(wt_name, [128, NE * HC * H], BF16)
    mg["acc"] = [P.sb(f"{wt_name}_acc{p}", [128, H], F32) for p in range(2)]
    mg["xob"] = [P.sb(f"{wt_name}_xob{p}", [128, H], F32) for p in range(2)]
    mg["b_s"] = P.sb(f"{wt_name}_b_s", [NE, H], F32)
    mg["gt_s"] = P.sb(f"{wt_name}_gt", [NE, 128], F32)
    mg["runbase"] = P.sb(f"{wt_name}_rb", [1, NE], F32)
    mg["xb2"] = [P.sb(f"{wt_name}_xb{p}", [128, H], F32) for p in range(2)]
    mg["xn2"] = [P.sb(f"{wt_name}_xn{p}", [128, H], BF16) for p in range(2)]
    mg["xnT2"] = [P.sb(f"{wt_name}_xnT{p}", [128, H], BF16) for p in range(2)]
    mg["st2"] = [{k: P.sb(f"{wt_name}_s{k}{p}", [128, 1], F32)
                  for k in ("mraw", "mean", "msq", "var", "lnv", "rstd",
                            "sums", "nb2")} for p in range(2)]
    mg["sxn"] = [P.sb(f"{wt_name}_sxn{p}", [128, H], BF16) for p in range(2)]
    mg["sxnT"] = [P.sb(f"{wt_name}_sxT{p}", [128, H], BF16) for p in range(2)]
    mg["sst"] = [{k: P.sb(f"{wt_name}_t{k}{p}", [128, 1], F32)
                  for k in ("mraw", "mean", "msq", "var", "lnv", "rstd",
                            "sums", "nb2")} for p in range(2)]
    gx = {}
    for k, s in (("tmp8", NE), ("Ls", 1), ("keep", 1), ("gg", 1)):
        gx[k] = P.sb(f"{wt_name}_g{k}", [128, s], F32)
    gx["G"] = [P.sb(f"{wt_name}_gG{p}", [128, NE], F32) for p in range(2)]
    mg["gat_x"] = gx
    return mg




def build_fused():
    P = Prog()
    nc = P.nc
    x_in = nc.dram_tensor("x", [SEQ, H], F32, kind="ExternalInput").ap()
    wq_in = nc.dram_tensor("wq", [128, 8192], F32, kind="ExternalInput").ap()
    wk_in = nc.dram_tensor("wk", [128, 8192], F32, kind="ExternalInput").ap()
    wv_in = nc.dram_tensor("wv", [128, 8192], F32, kind="ExternalInput").ap()
    wo_in = nc.dram_tensor("wo", [128, 8192], F32, kind="ExternalInput").ap()
    wg_in = nc.dram_tensor("wg", [128, HC * NE], F32, kind="ExternalInput").ap()
    wt_in = nc.dram_tensor("wt", [128, NE * HC * H], F32, kind="ExternalInput").ap()
    b_in = nc.dram_tensor("b", [NE, H], F32, kind="ExternalInput").ap()
    sel_in = nc.dram_tensor("sel", [NE, 1], F32, kind="ExternalInput").ap()
    myrow_in = nc.dram_tensor("myrow", [1, NE], F32, kind="ExternalInput").ap()
    y_out = nc.dram_tensor("y", [SEQ, H], F32, kind="ExternalOutput").ap()

    qt_hbm = nc.dram_tensor("qt_hbm", [NT, 128, 8, 128], BF16, kind="Internal").ap()
    xn2s_hbm = nc.dram_tensor("xn2s_hbm", [128, NT * H], BF16, kind="Internal").ap()
    xnT2s_hbm = nc.dram_tensor("xnT2s_hbm", [128, HC * SEQ], BF16,
                               kind="Internal").ap()
    x1_hbm = nc.dram_tensor("x1_hbm", [SEQ, H], F32, kind="Internal").ap()
    x2_hbm = nc.dram_tensor("x2_hbm", [SEQ, H], F32, kind="Internal").ap()
    x3_hbm = nc.dram_tensor("x3_hbm", [SEQ, H], F32, kind="Internal").ap()
    cc1i = nc.dram_tensor("cc1i", [NE, NE], F32)
    cc1o = nc.dram_tensor("cc1o", [NE, NE], F32)
    cc2i = nc.dram_tensor("cc2i", [NE, NE], F32)
    cc2o = nc.dram_tensor("cc2o", [NE, NE], F32)

    pl = Pl(nc)

    xbuf = [P.sb(f"xbuf{p}", [128, H], F32) for p in range(2)]
    sq = P.sb("sq", [128, H], BF16)
    xn = [P.sb(f"xn{p}", [128, H], BF16) for p in range(2)]
    stats = [{k: P.sb(f"st_{k}{p}", [128, 1], F32)
              for k in ("mraw", "mean", "msq", "var", "lnv", "rstd", "sums", "nb2")}
             for p in range(2)]
    onescol_s = P.sb("onescol", [128, 1], F32)
    onesrow_s = P.sb("onesrow", [1, 128], F32)
    identb_s = P.sb("identb_s", [128, 128], BF16)
    identf_s = P.sb("identf_s", [128, 128], F32)
    tri_s = P.sb("tri_s", [128, 128], F32)
    wgb = P.sb("wgb", [128, HC * NE], BF16)
    sel_s = P.sb("sel_s", [NE, 1], F32)
    myrowT_s = P.sb("myrowT_s", [1, NE], F32)
    cnts_s = P.sb("cnts_s", [NE, NE], F32)
    cnt_s = P.sb("cnt_s", [1, NE], F32)
    tbl_s = P.sb("tbl_s", [NE, NE], F32)
    xnT_t = [P.sb(f"xnTt{p}", [128, H], BF16) for p in range(2)]
    gat = {}
    for k, s in (("mx8", 1), ("oh", NE), ("nmx", 1), ("e8", NE), ("z8", 1),
                 ("gate0", 1)):
        gat[k] = P.sb("g_" + k, [128, s], F32)
    gat["onescol"] = onescol_s
    sh = {"xbuf": xbuf, "sq": sq, "xn": xn, "stats": stats, "xnT_t": xnT_t,
          "wgb": wgb, "sel_s": sel_s, "cnts_s": cnts_s, "tri_s": tri_s,
          "identf_s": identf_s, "onesrow_s": onesrow_s, "onescol_s": onescol_s,
          "identb_s": identb_s, "gat": gat}
    trail = {"xn": xn, "sq": sq, "stats": stats, "xnT": xnT_t, "wgb": wgb,
             "onescol": onescol_s, "cnt_s": cnt_s,
             "lgs": P.sb("tr_lgs", [128, NE], F32),
             "tmx": P.sb("tr_tmx", [128, 1], F32),
             "ohb": [P.sb(f"tr_oh{p}", [128, NE], F32) for p in range(2)]}

    for buf, src in ((identb_s, P.identb.ap()), (identf_s, P.identf.ap()),
                     (tri_s, P.c_triu.ap()), (onescol_s, P.c_onescol.ap()),
                     (onesrow_s, P.c_onesrow.ap()), (sel_s, sel_in),
                     (myrowT_s, myrow_in)):
        pl.dma("sync", lambda h, buf=buf, src=src: h.dma_start(out=buf[:], in_=src),
               "c")
    pl.dma("gpsimd", lambda h: h.dma_start(out=wgb[:], in_=wg_in), "wgl")
    pl.wait("tensor", "c")

    markA = len(P.guards)
    KT = P.sb("KT", [128, HC * SEQ], BF16)
    V = P.sb("Vb", [128, NT * H], BF16)
    markA1 = len(P.guards)
    wvb = P.sb("wvb", [128, 8192], BF16)
    wring = [P.sb(f"wring{p}", [128, 1024], BF16) for p in range(4)]
    xnTg = [P.sb(f"xnTg{p}", [128, HC * 512], BF16) for p in range(2)]
    qstage = P.sb("qstage", [128, 8 * 512], BF16)

    pl.dma("gpsimd", lambda h: h.dma_start(out=wvb[:], in_=wv_in), "wv")

    bank_free = {g: [("dve", 0)] for g in range(8)}
    xbuf_free = [[("dve", 0)]] * 2
    xn_free = [[("pe", 0)]] * 2
    xnTg_free = [("pe", 0), ("pe", 0)]
    qstage_free = ("qs", 0)
    ring_free = [("pe", 0)] * 4
    prev_act = [0, 0]

    for i in range(NT):
        par = i % 2
        g, m = i // 4, i % 4
        pl.wait_many("sync", xbuf_free[par])
        sld = pl.dma("sync", lambda h, i=i, par=par: h.dma_start(
            out=xbuf[par][:], in_=x_in[i * 128:(i + 1) * 128, :]), "x%d" % par)
        xn_evt, xfree, pa = emit_ln(P, pl, xbuf[par][:], xn[par][:], sq[:], stats[par],
                                    [("x%d" % par, sld)] + xn_free[par], prev_act[par])
        prev_act[par] = pa
        xbuf_free[par] = list(xfree)
        gpar = g % 2
        extra = [xnTg_free[gpar]] if m == 0 else []
        pe_t = emit_transpose8(P, pl, lambda c, par=par: xn[par][:, c * 128:(c + 1) * 128],
                               6, xn_evt, bank_free[6] + extra, identb_s)
        xn_free[par] = [("pe", pe_t)]
        pl.wait("vector", "pe", pe_t)
        dst = xnTg[gpar][:].rearrange("p (c t) -> p c t", c=HC)[:, :, m * 128:(m + 1) * 128]
        src = P.bankb(6)[:].rearrange("p (c q) -> p c q", c=HC)
        dev = pl.do("vector", lambda h, dst=dst, src=src: h.tensor_copy(dst, src),
                    inc="dve")
        bank_free[6] = [("dve", dev)]
        pl.wait("tensor", "dve", dev)
        if i == 0:
            pl.wait("tensor", "wv")
        for bk in (0, 1):
            pl.wait_many("tensor", bank_free[bk])
        pev = {}
        for half in range(2):
            for c in range(HC):
                pev[half] = pl.do("tensor", lambda h, half=half, c=c, m=m, gpar=gpar:
                                  h.matmul(
                    P.bankf(half),
                    xnTg[gpar][:, c * 512 + m * 128: c * 512 + (m + 1) * 128],
                    wvb[:, c * 1024 + half * 512: c * 1024 + (half + 1) * 512],
                    start=(c == 0), stop=(c == HC - 1)),
                    inc="pe" if c == HC - 1 else None)
        for half in range(2):
            pl.wait("vector", "pe", pev[half])
            dv = pl.do("vector", lambda h, half=half, i=i: h.tensor_copy(
                V[:, i * 1024 + half * 512: i * 1024 + (half + 1) * 512],
                P.bankf(half)), inc="dve")
            bank_free[half] = [("dve", dv)]

        if m == 3:
            seq16 = [("q", wq_in, s) for s in range(8)] + \
                    [("k", wk_in, s) for s in range(8)]

            def wload(t):
                which, win, s = seq16[t]
                slot = t % 4
                pl.wait_many("gpsimd", [ring_free[slot]])
                pl.dma("gpsimd", lambda h, win=win, s=s, slot=slot: h.dma_start(
                    out=wring[slot][:], in_=win[:, s * 1024:(s + 1) * 1024]),
                    "wr%d" % slot)

            for t in range(4):
                wload(t)
            for t in range(16):
                which, win, s = seq16[t]
                slot = t % 4
                pl.wait("tensor", "wr%d" % slot)
                pl.wait_many("tensor", bank_free[s])
                if which == "q" and s == 0:
                    pl.wait_many("tensor", [qstage_free])
                pes = None
                for c in range(HC):
                    pes = pl.do("tensor", lambda h, s=s, c=c, slot=slot, gpar=gpar:
                                h.matmul(
                        P.bankf(s), wring[slot][:, c * 128:(c + 1) * 128],
                        xnTg[gpar][:, c * 512:(c + 1) * 512],
                        start=(c == 0), stop=(c == HC - 1)),
                        inc="pe" if c == HC - 1 else None)
                ring_free[slot] = ("pe", pes)
                if t + 4 < 16:
                    wload(t + 4)
                pl.wait("vector", "pe", pes)
                if which == "q":
                    pl.wait_many("vector", [qstage_free])
                    dv = pl.do("vector", lambda h, s=s: h.tensor_copy(
                        qstage[:, s * 512:(s + 1) * 512], P.bankf(s)), inc="dve")
                else:
                    dv = pl.do("vector", lambda h, s=s, g=g: h.tensor_copy(
                        KT[:, s * SEQ + g * 512: s * SEQ + (g + 1) * 512],
                        P.bankf(s)), inc="dve")
                bank_free[s] = [("dve", dv)]
                if which == "q" and s == 7:
                    pl.wait("gpsimd", "dve", pl.cnt["dve"])
                    gpv = None
                    for mm in range(4):
                        i2 = 4 * g + mm
                        srcq = qstage[:].rearrange("p (s q) -> p s q", s=8)[
                            :, :, mm * 128:(mm + 1) * 128]
                        gpv = pl.dma("gpsimd", lambda h, i2=i2, srcq=srcq: h.dma_start(
                            out=qt_hbm[i2], in_=srcq), "qs")
                    qstage_free = ("qs", gpv)
            xnTg_free[gpar] = ("pe", pl.cnt["pe"])

    pl.join()
    P.free_to(markA1)

    wob = P.sb("wob", [128, 8192], BF16)
    qbuf = [P.sb(f"qbuf{p}", [128, 1024], BF16) for p in range(2)]
    bufs = {
        "Abuf": P.sb("Abuf", [128, SEQ], BF16),
        "ATb": P.sb("ATb", [128, SEQ], BF16),
        "ctxb": P.sb("ctxb", [128, H], BF16),
        "ctxT": P.sb("ctxT", [128, H], BF16),
        "cbias": P.sb("cbias", [128, 1], F32),
        "zp": P.sb("zp", [128, 8], F32),
        "sm": {k: P.sb("sm_" + k, [128, 1], F32) for k in ("zs", "rz")},
        "xbuf": xbuf,
        "outb": [P.sb(f"x1b{p}", [128, H], F32) for p in range(2)],
        "wob": wob,
    }
    pl.dma("gpsimd", lambda h: h.dma_start(out=wob[:], in_=wo_in), "wo")
    pl.dma("sync", lambda h: h.dma_start(out=bufs["cbias"][:], in_=P.c_zero1.ap()), "c")
    pl.wait("tensor", "wo")
    pl.wait("scalar", "c")

    def qload(pl, j, par):
        pl.dma("sync", lambda h, j=j, par=par: h.dma_start(
            out=qbuf[par][:], in_=qt_hbm[j].rearrange("p s q -> p (s q)")),
            "q%d" % par)
        return "q%d" % par

    bufs["qload"] = qload
    emit_attention(
        P, pl, bufs,
        qsrc=lambda c, j, par: qbuf[par][:, c * 128:(c + 1) * 128],
        ksrc=lambda c, gb: KT[:, c * SEQ + gb * 512: c * SEQ + (gb + 1) * 512],
        vsrc=lambda kc, half: V[:, kc * 1024 + half * 512: kc * 1024 + (half + 1) * 512],
        xres_hbm=x_in, dst_hbm=x1_hbm, scale=1.0 / 32.0, oproj=True,
        identb_s=identb_s, trail=trail)

    pl.join()
    P.free_to(markA)

    markM = len(P.guards)
    mg1 = alloc_moe_group(P, "wt1")
    for e in range(NE):
        pl.dma("gpsimd", lambda h, e=e: h.dma_start(
            out=mg1["wtb"][:, e * 8192:(e + 1) * 8192],
            in_=wt_in[:, e * 8192:(e + 1) * 8192]), "wt%d" % e)
    pl.dma("sync", lambda h: h.dma_start(out=mg1["b_s"][:], in_=b_in), "c")
    emit_collective(P, pl, myrowT_s, cnt_s, tbl_s, cc1i, cc1o, cnts_s)

    emit_moe(P, pl, sh, mg1, x1_hbm, x2_hbm, mg1["wtb"])
    pl.join()
    P.free_to(markM)

    markB = len(P.guards)
    xn2 = P.sb("xn2", [128, NT * H], BF16)
    xnT2 = P.sb("xnT2", [128, HC * SEQ], BF16)
    bufsB = {
        "Abuf": P.sb("Abuf2", [128, SEQ], BF16),
        "ATb": P.sb("ATb2", [128, SEQ], BF16),
        "cbias": P.sb("cbias2", [128, 1], F32),
        "zp": P.sb("zp2", [128, 8], F32),
        "sm": {k: P.sb("sm2_" + k, [128, 1], F32) for k in ("zs", "rz")},
        "xbuf": xbuf,
        "outb": [P.sb(f"x3b{p}", [128, H], F32) for p in range(2)],
    }
    pl.dma("sync", lambda h: h.dma_start(out=bufsB["cbias"][:], in_=P.c_m1024.ap()),
           "c")
    pl.wait("scalar", "c")
    xb4 = xbuf + [P.sb(f"xbufB{p}", [128, H], F32) for p in range(2)]
    st4 = stats + [{k: P.sb(f"stB_{k}{p}", [128, 1], F32)
                    for k in ("mraw", "mean", "msq", "var", "lnv", "rstd",
                              "sums", "nb2")} for p in range(2)]
    prev_act = [pl.cnt["act"]] * 4
    xbuf_free = [[("dve", pl.cnt["dve"])]] * 4
    bk_free = [[("dve", pl.cnt["dve"])]] * 2
    for i in range(NT):
        par = i % 4
        bk = i % 2
        pl.wait_many("sync", xbuf_free[par])
        sld = pl.dma("sync", lambda h, i=i, par=par: h.dma_start(
            out=xb4[par][:], in_=x2_hbm[i * 128:(i + 1) * 128, :]), "x%d" % par)
        xn_evt, xfree, pa = emit_ln(P, pl, xb4[par][:], xn2[:, i * 1024:(i + 1) * 1024],
                                    sq[:], st4[par], [("x%d" % par, sld)],
                                    prev_act[par])
        prev_act[par] = pa
        xbuf_free[par] = list(xfree)
        pe_t = emit_transpose8(
            P, pl,
            lambda c, i=i: xn2[:, i * 1024 + c * 128: i * 1024 + (c + 1) * 128],
            bk, xn_evt, bk_free[bk], identb_s)
        pl.wait("vector", "pe", pe_t)
        dst = xnT2[:].rearrange("p (c t) -> p c t", c=HC)[:, :, i * 128:(i + 1) * 128]
        src = P.bankb(bk)[:].rearrange("p (c q) -> p c q", c=HC)
        dev = pl.do("vector", lambda h, dst=dst, src=src: h.tensor_copy(dst, src),
                    inc="dve")
        bk_free[bk] = [("dve", dev)]
    pl.join()

    emit_attention(
        P, pl, bufsB,
        qsrc=lambda c, j, par: xnT2[:, c * SEQ + j * 128: c * SEQ + (j + 1) * 128],
        ksrc=lambda c, gb: xnT2[:, c * SEQ + gb * 512: c * SEQ + (gb + 1) * 512],
        vsrc=lambda kc, half: xn2[:, kc * 1024 + half * 512: kc * 1024 + (half + 1) * 512],
        xres_hbm=x2_hbm, dst_hbm=x3_hbm, scale=1.0, oproj=False,
        identb_s=identb_s, trail=trail)

    pl.join()
    P.free_to(markB)

    markM2 = len(P.guards)
    mg2 = alloc_moe_group(P, "wt2")
    for e in range(NE):
        pl.dma("gpsimd", lambda h, e=e: h.dma_start(
            out=mg2["wtb"][:, e * 8192:(e + 1) * 8192],
            in_=wt_in[:, e * 8192:(e + 1) * 8192]), "wt%d" % e)
    pl.dma("sync", lambda h: h.dma_start(out=mg2["b_s"][:], in_=b_in), "c")
    emit_collective(P, pl, myrowT_s, cnt_s, tbl_s, cc2i, cc2o, cnts_s)

    emit_moe(P, pl, sh, mg2, x3_hbm, y_out, mg2["wtb"])

    for strm in ("so0", "so1"):
        pl.wait("gpsimd", strm)
        pl.wait("sync", strm)

    block = nc.Block()
    with block:
        pl.play(block)
    return nc



_CACHE = {}


def _get_prog():
    if "p" not in _CACHE:
        _CACHE["p"] = build_fused()
    return _CACHE["p"]


def kernel(L0_attn_a_x, L0_attn_a_W_qkv, L0_attn_a_W_out, L0_ffn_wg, L0_ffn_W,
           L0_ffn_b):
    x = np.ascontiguousarray(np.asarray(L0_attn_a_x, dtype=np.float32))
    Wqkv = np.asarray(L0_attn_a_W_qkv, dtype=np.float32)
    Wq, Wk, Wv = Wqkv[0::3], Wqkv[1::3], Wqkv[2::3]
    st = stage_weights(Wq, Wk, Wv, np.asarray(L0_attn_a_W_out, np.float32),
                       np.asarray(L0_ffn_wg, np.float32),
                       np.asarray(L0_ffn_W, np.float32),
                       np.asarray(L0_ffn_b, np.float32))
    cores = list(range(8))
    prog = _get_prog()

    ins = []
    for c in cores:
        sel = np.ascontiguousarray(
            (np.arange(NE) < c).astype(np.float32).reshape(NE, 1))
        myrow = np.zeros((1, NE), np.float32)
        myrow[0, c] = 1.0
        ins.append({"x": np.ascontiguousarray(x[c]), "wq": st["wq"],
                    "wk": st["wk"], "wv": st["wv"], "wo": st["wo"],
                    "wg": st["wg"], "wt": st["wt"], "b": st["b"],
                    "sel": sel, "myrow": myrow})
    r = run_bass_kernel_spmd(prog, ins, core_ids=cores)
    out = np.stack([r.results[c]["y"] for c in cores]).astype(np.float32)
    return out


# version 22
# speedup vs baseline: 1.0108x; 1.0108x over previous
import numpy as np
import ml_dtypes
import concourse.bass as bass
import concourse.mybir as mybir
from concourse.bass_utils import run_bass_kernel_spmd

F32 = mybir.dt.float32
BF16 = mybir.dt.bfloat16
AF = mybir.ActivationFunctionType
ALU = mybir.AluOpType
AX = mybir.AxisListType
BF = ml_dtypes.bfloat16

SEQ = 4096
H = 1024
NT = SEQ // 128
HC = H // 128
NE = 8
CAP = float(SEQ * 8 // NE)
EPS = 1e-5
ENGINES = ("sync", "gpsimd", "tensor", "vector", "scalar")


class Pl:

    ENG_SEMS = ("pe", "dve", "act")

    def __init__(self, nc):
        self.nc = nc
        self.q = {e: [] for e in ENGINES}
        self.sems = {n: nc.alloc_semaphore("s_" + n) for n in self.ENG_SEMS}
        self.cnt = {n: 0 for n in self.ENG_SEMS}
        self.lastw = {}
        self.g_lastoh = 0
        self.g_laste8 = 0

    def _stream(self, name):
        if name not in self.sems:
            self.sems[name] = self.nc.alloc_semaphore("s_" + name)
            self.cnt[name] = 0
        return self.sems[name]

    def dma(self, eng, fn, stream):
        s = self._stream(stream)
        self.cnt[stream] += 16
        self.q[eng].append(lambda h, fn=fn, s=s: fn(h).then_inc(s, 16))
        return self.cnt[stream]

    def cc(self, fn, stream):
        s = self._stream(stream)
        self.cnt[stream] += 1
        self.q["gpsimd"].append(lambda h, fn=fn, s=s: fn(h).then_inc(s, 1))
        return self.cnt[stream]

    def wait(self, eng, name, val=None):
        if name not in self.ENG_SEMS:
            self._stream(name)
            val = None
        v = self.cnt[name] if val is None else val
        if v <= 0:
            return
        key = (eng, name)
        if self.lastw.get(key, -1) >= v:
            return
        self.lastw[key] = v
        s = self.sems[name]
        self.q[eng].append(lambda h, s=s, v=v: h.wait_ge(s, v))

    def wait_many(self, eng, pairs):
        for nm, v in pairs:
            self.wait(eng, nm, v)

    def waitv(self, eng, name, v):
        if v <= 0:
            return
        key = (eng, name)
        if self.lastw.get(key, -1) >= v:
            return
        self.lastw[key] = v
        s = self._stream(name)
        self.q[eng].append(lambda h, s=s, v=v: h.wait_ge(s, v))

    AUTO = {"vector": "dve", "scalar": "act"}

    def do(self, eng, fn, inc=None):
        auto = self.AUTO.get(eng)
        if auto is not None:
            self.wait(eng, auto, self.cnt[auto])
            if inc is None:
                inc = auto
            assert inc == auto
        if inc is None:
            self.q[eng].append(fn)
            return None
        s = self.sems[inc]
        self.cnt[inc] += 1
        self.q[eng].append(lambda h, fn=fn, s=s: fn(h).then_inc(s, 1))
        return self.cnt[inc]

    def join(self, engs=ENGINES, skip=()):
        snap = dict(self.cnt)
        for e in engs:
            for n, v in snap.items():
                if n in skip:
                    continue
                self.wait(e, n, v)

    def play(self, block):
        for eng in ENGINES:
            ops = self.q[eng]
            if not ops:
                continue
            getattr(block, eng)(lambda h, ops=ops: [f(h) for f in ops])


class Prog:
    def __init__(self):
        self.nc = bass.Bass("TRN2", target_bir_lowering=False)
        nc = self.nc
        self.ps = nc.alloc_psum_tensor("ps", [128, 4096], F32)
        self.identb = nc.inline_tensor(np.eye(128, dtype=BF), "c_identb")
        self.identf = nc.inline_tensor(np.eye(128, dtype=np.float32), "c_identf")
        self.c_triu = nc.inline_tensor(
            np.triu(np.ones((128, 128), np.float32), 1), "c_triu")
        self.c_onescol = nc.inline_tensor(np.ones((128, 1), np.float32), "c_onescol")
        self.c_onesrow = nc.inline_tensor(np.ones((1, 128), np.float32), "c_onesrow")
        self.c_zero1 = nc.inline_tensor(np.zeros((128, 1), np.float32), "c_zero1")
        self.c_m1024 = nc.inline_tensor(
            np.full((128, 1), -1024.0, np.float32), "c_m1024")
        self.guards = []

    def sb(self, name, shape, dtype):
        g = self.nc.sbuf_tensor(name, shape, dtype)
        t = g.__enter__()
        self.guards.append(g)
        return t

    def free_to(self, mark):
        while len(self.guards) > mark:
            self.guards.pop().__exit__(None, None, None)

    def bankf(self, g, n=512):
        return self.ps[:, g * 512:g * 512 + n]

    def bankb(self, g):
        v = self.ps[:, g * 512:(g + 1) * 512].bitcast(BF16)
        assert tuple(v.shape) == (128, 1024), v.shape
        return v


def emit_ln(P, pl, xsrc, xn_dst, sq, st, wait_src, prev_act):
    pl.wait_many("vector", wait_src)
    pl.wait_many("scalar", wait_src)
    pl.wait("vector", "act", prev_act)
    pl.do("vector", lambda h: h.reduce_sum(st["mraw"][:], xsrc, axis=AX.X))
    pl.do("vector", lambda h: h.scalar_tensor_tensor(
        sq, xsrc, 1.0, xsrc, op0=ALU.mult, op1=ALU.mult, accum_out=st["sums"][:]))
    dm = pl.do("vector", lambda h: h.tensor_scalar(
        st["mean"][:], st["mraw"][:], 1.0 / 1024.0, 0.0, op0=ALU.mult, op1=ALU.add),
        inc="dve")
    pl.wait("vector", "dve", dm)
    pl.do("vector", lambda h: h.tensor_scalar(
        st["msq"][:], st["mean"][:], st["mean"][:, 0:1], -EPS,
        op0=ALU.mult, op1=ALU.add))
    d = pl.do("vector", lambda h: h.scalar_tensor_tensor(
        st["var"][:], st["sums"][:], 1.0 / 1024.0, st["msq"][:],
        op0=ALU.mult, op1=ALU.subtract), inc="dve")
    pl.wait("scalar", "dve", d)
    pl.do("scalar", lambda h: h.activation(st["lnv"][:], st["var"][:], AF.Ln,
                                           scale=1.0))
    a = pl.do("scalar", lambda h: h.activation(st["rstd"][:], st["lnv"][:], AF.Exp,
                                               scale=-0.5), inc="act")
    pl.wait("vector", "act", a)
    dnb = pl.do("vector", lambda h: h.tensor_scalar(
        st["nb2"][:], st["mean"][:], st["rstd"][:, 0:1], -1.0,
        op0=ALU.mult, op1=ALU.mult), inc="dve")
    pl.wait("scalar", "dve", dnb)
    d2 = pl.do("scalar", lambda h: h.activation(
        xn_dst, xsrc, AF.Identity, bias=st["nb2"][:, 0:1],
        scale=st["rstd"][:, 0:1]), inc="act")
    return ("act", d2), [("act", d2), ("dve", dnb)], d2


def emit_transpose8(P, pl, src_cols, bank, wait_src, free_at, identb_s):
    tgt = P.bankb(bank)
    pl.wait_many("tensor", [wait_src])
    pl.wait_many("tensor", free_at)
    pe = None
    for c in range(HC):
        pe = pl.do("tensor", lambda h, c=c: h.transpose(
            tgt[:, c * 128:(c + 1) * 128], src_cols(c), identb_s[:]),
            inc="pe" if c == HC - 1 else None)
    return pe


def emit_tr8_dma(P, pl, eng, src_col, dst_col, stream, waits):
    pl.wait_many(eng, waits)
    mk = None
    for c in range(HC):
        mk = pl.dma(eng, lambda h, c=c: h.dma_start(
            dst_col(c), src_col(c), transpose=True), stream)
    return mk


def emit_gating(P, pl, i, xnT_col, wgb, counts_only, gat, b1_free=()):
    lg = P.bankf(1, 512)[:, 0:NE]
    pl.wait("tensor", "dve", pl.g_lastoh)
    pl.wait("tensor", "act", pl.g_laste8)
    pl.wait_many("tensor", list(b1_free))
    pe = None
    for c in range(HC):
        pe = pl.do("tensor", lambda h, c=c: h.matmul(
            lg, xnT_col(c), wgb[:, c * NE:(c + 1) * NE],
            start=(c == 0), stop=(c == HC - 1)),
            inc="pe" if c == HC - 1 else None)
    pl.wait("vector", "pe", pe)
    d = pl.do("vector", lambda h: h.reduce_max(gat["mx8"][:], lg, axis=AX.X), inc="dve")
    pl.wait("vector", "dve", d)
    doh = pl.do("vector", lambda h: h.tensor_scalar(
        gat["oh"][:], lg, gat["mx8"][:, 0:1], 0.0, op0=ALU.subtract, op1=ALU.is_ge),
        inc="dve")
    pl.g_lastoh = doh
    if not counts_only:
        dn = pl.do("vector", lambda h: h.tensor_scalar(
            gat["nmx"][:], gat["mx8"][:], -1.0, 0.0, op0=ALU.mult, op1=ALU.add),
            inc="dve")
        pl.wait("scalar", "dve", dn)
        a = pl.do("scalar", lambda h: h.activation(
            gat["e8"][:], lg, AF.Exp, bias=gat["nmx"][:, 0:1], scale=1.0,
            accum_out=gat["z8"][:]), inc="act")
        pl.g_laste8 = a
        pl.wait("vector", "act", a)
        pl.do("vector", lambda h: h.reciprocal(gat["gate0"][:], gat["z8"][:]))
    pecnt = None
    if counts_only:
        pl.wait("tensor", "dve", doh)
        pecnt = pl.do("tensor", lambda h, i=i: h.matmul(
            P.bankf(3, 512)[0:1, 0:NE], gat["onescol"][:], gat["oh"][:],
            start=(i == 0), stop=(i == NT - 1)), inc="pe")
    return doh, pecnt


def emit_count_phase(P, pl, src_hbm, xbuf, sq, xn, stats, xnT_t, wgb, gat,
                     cnt_s, identb_s):
    xbuf_free = [[("dve", pl.cnt["dve"])]] * 2
    xn_free = [[("pe", pl.cnt["pe"])]] * 2
    xnTt_free = [[("pe", pl.cnt["pe"])]] * 2
    b0_free = [("dve", pl.cnt["dve"])]
    prev_act = [pl.cnt["act"], pl.cnt["act"]]
    for i in range(NT):
        par = i % 2
        pl.wait_many("sync", xbuf_free[par])
        sld = pl.dma("sync", lambda h, i=i, par=par: h.dma_start(
            out=xbuf[par][:], in_=src_hbm[i * 128:(i + 1) * 128, :]), "x%d" % par)
        xn_evt, xfree, pa = emit_ln(P, pl, xbuf[par][:], xn[par][:], sq[:], stats[par],
                                    [("x%d" % par, sld)] + xn_free[par], prev_act[par])
        prev_act[par] = pa
        xbuf_free[par] = list(xfree)
        pe_t = emit_transpose8(P, pl, lambda c, par=par: xn[par][:, c * 128:(c + 1) * 128],
                               0, xn_evt, b0_free + xnTt_free[par], identb_s)
        xn_free[par] = [("pe", pe_t)]
        pl.wait("vector", "pe", pe_t)
        dev = pl.do("vector", lambda h, par=par: h.tensor_copy(
            xnT_t[par][:], P.bankb(0)), inc="dve")
        b0_free = [("dve", dev)]
        pl.wait("tensor", "dve", dev)
        doh, pecnt = emit_gating(
            P, pl, i, lambda c, par=par: xnT_t[par][:, c * 128:(c + 1) * 128],
            wgb, True, gat)
        xnTt_free[par] = [("pe", pecnt)]
    pl.wait("vector", "pe", pl.cnt["pe"])
    dcv = pl.do("vector", lambda h: h.tensor_copy(
        cnt_s[:], P.bankf(3, 512)[0:1, 0:NE]), inc="dve")
    return dcv


def emit_collective(P, pl, myrowT_s, cnt_s, tbl_s, cc_in, cc_out, cnts_s):
    pe = pl.do("tensor", lambda h: h.matmul(
        P.bankf(2, 512)[0:NE, 0:NE], myrowT_s[:], cnt_s[:],
        start=True, stop=True), inc="pe")
    pl.wait("vector", "pe", pe)
    d = pl.do("vector", lambda h: h.tensor_copy(
        tbl_s[:], P.bankf(2, 512)[0:NE, 0:NE]), inc="dve")
    pl.wait("scalar", "dve", d)
    pl.dma("scalar", lambda h: h.dma_start(out=cc_in[:, :], in_=tbl_s[:]), "ccd")
    pl.wait("gpsimd", "ccd")
    pl.cc(lambda h: h.collective_compute(
        "AllReduce", ALU.add, replica_groups=[list(range(8))],
        ins=[cc_in.ap().opt()], outs=[cc_out.ap().opt()]), "ccc")
    pl.wait("gpsimd", "ccc")
    pl.dma("gpsimd", lambda h: h.dma_start(out=cnts_s[:], in_=cc_out[:, :]), "ccr")
    return d


def stage_weights(Wq, Wk, Wv, Wo, wg, W, b):
    def qk(WT):
        return np.ascontiguousarray(
            WT.reshape(HC, 128, 8, 128).transpose(1, 2, 0, 3).reshape(128, 8192))

    def vo(WT):
        return np.ascontiguousarray(
            WT.reshape(HC, 128, H).transpose(1, 0, 2).reshape(128, 8192))

    out = {}
    out["wq"] = qk(np.ascontiguousarray(Wq.T))
    out["wk"] = qk(np.ascontiguousarray(Wk.T))
    out["wv"] = vo(np.ascontiguousarray(Wv.T))
    out["wo"] = vo(np.ascontiguousarray(Wo.T))
    out["wg"] = np.ascontiguousarray(
        wg.reshape(HC, 128, NE).transpose(1, 0, 2).reshape(128, HC * NE))
    WT = W.transpose(0, 2, 1)
    out["wt"] = np.ascontiguousarray(
        WT.reshape(NE, HC, 128, H).transpose(2, 0, 1, 3).reshape(128, NE * HC * H))
    out["b"] = np.ascontiguousarray(b)
    return out


def emit_attention(P, pl, bufs, qsrc, ksrc, vsrc, xres_hbm, dst_hbm, scale,
                   oproj, identb_s, trail=None):
    xbuf = bufs["xbuf"]
    outb = bufs["outb"]
    Abuf, ATb = bufs["Abuf"], bufs["ATb"]
    zp, sm = bufs["zp"], bufs["sm"]
    cbias = bufs["cbias"]
    qload = bufs.get("qload")
    kvm = bufs.get("kv_marks")

    xbuf_free = [[("dve", pl.cnt["dve"])]] * 2
    outb_free = [[("so0", 0)], [("so1", 0)]]
    qbuf_free = [[("pe", pl.cnt["pe"])]] * 2
    bank_free = {g: [("dve", pl.cnt["dve"]), ("act", pl.cnt["act"])]
                 for g in range(8)}
    st8 = {}

    T = trail
    if T is not None:
        tr_xnfree = [[("xt", 0)], [("xt", 0)]]
        tr_prev = [pl.cnt["act"], pl.cnt["act"]]
        tr_mark = {}
        tr_cmm = {}
        tr_doh = {}
        tr_last = {"B": None}
        pl.wait("tensor", "wgl")

    def trail_A(t2, ppar2, dxo2):
        par = t2 % 2
        xn_evt, xfree, pa = emit_ln(P, pl, outb[ppar2][:], T["xn"][par][:],
                                    T["sq"][:], T["stats"][par],
                                    [("dve", dxo2)] + tr_xnfree[par],
                                    tr_prev[par])
        tr_prev[par] = pa
        outb_free[ppar2].extend(xfree)
        war = [("pe", tr_cmm[t2 - 2])] if t2 - 2 in tr_cmm else []
        mk = emit_tr8_dma(P, pl, "sync",
                          lambda c, par=par: T["xn"][par][:, c * 128:(c + 1) * 128],
                          lambda c, par=par: T["xnT"][par][:, c * 128:(c + 1) * 128],
                          "xt", [xn_evt] + war)
        tr_mark[t2] = mk
        tr_xnfree[par] = [("xt", mk)]

    def trail_B(tB, at_ev0):
        pl.waitv("tensor", "xt", tr_mark[tB])
        pl.wait("tensor", "dve",
                at_ev0 if at_ev0 is not None else pl.cnt["dve"])
        if tr_last["B"] is not None:
            pl.wait("tensor", "dve", tr_doh[tr_last["B"]])
        par = tB % 2
        cm = None
        for c in range(HC):
            cm = pl.do("tensor", lambda h, c=c, par=par: h.matmul(
                P.bankf(0, 512)[:, 0:NE],
                T["xnT"][par][:, c * 128:(c + 1) * 128],
                T["wgb"][:, c * NE:(c + 1) * NE],
                start=(c == 0), stop=(c == HC - 1)),
                inc="pe" if c == HC - 1 else None)
        tr_cmm[tB] = cm
        pl.wait("vector", "pe", cm)
        pl.do("vector", lambda h: h.tensor_copy(
            T["lgs"][:], P.bankf(0, 512)[:, 0:NE]))
        d1 = pl.do("vector", lambda h: h.reduce_max(
            T["tmx"][:], T["lgs"][:], axis=AX.X), inc="dve")
        pl.wait("vector", "dve", d1)
        pl.do("vector", lambda h: h.tensor_scalar(
            T["ohb"][0][:], T["lgs"][:], T["tmx"][:, 0:1], 0.0,
            op0=ALU.subtract, op1=ALU.is_ge))
        if tB == 0:
            doh = pl.do("vector", lambda h: h.tensor_copy(
                T["ohsum"][:], T["ohb"][0][:]), inc="dve")
        else:
            doh = pl.do("vector", lambda h: h.tensor_add(
                T["ohsum"][:], T["ohsum"][:], T["ohb"][0][:]), inc="dve")
        tr_doh[tB] = doh
        tr_last["B"] = tB
        return [("dve", doh)]

    def trail_fin():
        pl.wait("tensor", "dve", tr_doh[NT - 1])
        pc = pl.do("tensor", lambda h: h.matmul(
            P.bankf(0, 512)[0:1, 16:16 + NE], T["onescol"][:], T["ohsum"][:],
            start=True, stop=True), inc="pe")
        pl.wait("vector", "pe", pc)
        pl.do("vector", lambda h: h.tensor_copy(
            T["cnt_s"][:], P.bankf(0, 512)[0:1, 16:16 + NE]), inc="dve")

    for j in range(NT + 1):
        par = j % 2
        ppar = (j - 1) % 2
        t = st8.get(j - 1)
        if j < NT:
            if qload is not None:
                pl.wait_many("sync", qbuf_free[par])
                qstrm = qload(pl, j, par)
            else:
                qstrm = None
            pl.wait_many("sync", xbuf_free[par])
            pl.dma("sync", lambda h, j=j, par=par: h.dma_start(
                out=xbuf[par][:], in_=xres_hbm[j * 128:(j + 1) * 128, :]),
                "x%d" % par)
        if t is not None:
            pl.wait("vector", "act", t["exp"][7])
            pl.do("vector", lambda h: h.reduce_sum(sm["zs"][:], zp[:], axis=AX.X))
            drz = pl.do("vector", lambda h: h.reciprocal(sm["rz"][:], sm["zs"][:]),
                        inc="dve")
            pe_tr = {}
            at_ev = {}
            pe_cx = {}
            for bk in (4, 5):
                pl.wait_many("tensor", bank_free[bk])
            pl.wait("tensor", "act", t["exp"][5])
            if kvm is not None:
                pl.wait("tensor", "pkv")

            def emit_tr(tb):
                for kc in range(tb * 8, tb * 8 + 8):
                    pl.wait("tensor", "act", t["exp"][kc // 4])
                    pl.wait("tensor", "act", t["exp"][tb])
                    pe_tr[tb] = pl.do("tensor", lambda h, kc=kc, tb=tb: h.transpose(
                        P.bankb(tb)[:, (kc % 8) * 128:(kc % 8 + 1) * 128],
                        Abuf[:, kc * 128:(kc + 1) * 128], identb_s[:]),
                        inc="pe" if kc % 8 == 7 else None)
                pl.wait("vector", "pe", pe_tr[tb])
                at_ev[tb] = pl.do("vector", lambda h, tb=tb: h.tensor_copy(
                    ATb[:, tb * 1024:(tb + 1) * 1024], P.bankb(tb)), inc="dve")
                bank_free[tb] = [("dve", at_ev[tb])]

            emit_tr(0)
            for tb in range(4):
                if tb + 1 < 4:
                    emit_tr(tb + 1)
                pl.wait("tensor", "dve", at_ev[tb])
                for kc in range(tb * 8, tb * 8 + 8):
                    for half in range(2):
                        pe_cx[half] = pl.do("tensor", lambda h, kc=kc, half=half:
                                            h.matmul(
                            P.bankf(4 + half), ATb[:, kc * 128:(kc + 1) * 128],
                            vsrc(kc, half),
                            start=(kc == 0), stop=(kc == NT - 1)),
                            inc="pe" if kc == NT - 1 else None)
                if tb == 0 and T is not None and j - 3 >= 0:
                    bank_free[0] = bank_free[0] + trail_B(j - 3, at_ev[0])
            if oproj:
                ctxb, ctxT, wob = bufs["ctxb"], bufs["ctxT"], bufs["wob"]
                pl.wait("scalar", "dve", drz)
                cx_ev = None
                for half in range(2):
                    pl.wait("scalar", "pe", pe_cx[half])
                    cx_ev = pl.do("scalar", lambda h, half=half: h.activation(
                        ctxb[:, half * 512:(half + 1) * 512], P.bankf(4 + half),
                        AF.Identity, scale=sm["rz"][:, 0:1]), inc="act")
                    bank_free[4 + half] = [("act", cx_ev)]
                pe_ct = emit_transpose8(
                    P, pl, lambda c: ctxb[:, c * 128:(c + 1) * 128],
                    6, ("act", cx_ev), bank_free[6], identb_s)
                pl.wait("scalar", "pe", pe_ct)
                ct_ev = pl.do("scalar", lambda h: h.activation(
                    ctxT[:], P.bankb(6), AF.Identity), inc="act")
                bank_free[6] = [("act", ct_ev)]
            else:
                pl.wait_many("vector", [("x%d" % ppar, 0)])
                pl.wait_many("vector", outb_free[ppar])
                dxo = None
                for half in range(2):
                    pl.wait("vector", "pe", pe_cx[half])
                    dxo = pl.do("vector", lambda h, half=half, ppar=ppar:
                                h.scalar_tensor_tensor(
                        outb[ppar][:, half * 512:(half + 1) * 512],
                        P.bankf(4 + half), sm["rz"][:, 0:1],
                        xbuf[ppar][:, half * 512:(half + 1) * 512],
                        op0=ALU.mult, op1=ALU.add), inc="dve")
                    bank_free[4 + half] = [("dve", dxo)]
                xbuf_free[ppar] = [("dve", dxo)]
                pl.wait("gpsimd", "dve", dxo)
                gst = pl.dma("gpsimd", lambda h, j=j, ppar=ppar: h.dma_start(
                    out=dst_hbm[(j - 1) * 128:j * 128, :], in_=outb[ppar][:]),
                    "so%d" % ppar)
                outb_free[ppar] = [("so%d" % ppar, gst)]
                t["dxo"] = dxo
                t["ppar"] = ppar
        if j < NT:
            if qstrm is not None:
                pl.wait("tensor", qstrm)
            pe_s = {}
            for gb in range(6):
                pl.wait_many("tensor", bank_free[gb])
                if kvm is not None:
                    pl.waitv("tensor", "pk", kvm[gb])
                for c in range(HC):
                    pe_s[gb] = pl.do("tensor", lambda h, c=c, gb=gb, j=j, par=par:
                                     h.matmul(
                        P.bankf(gb), qsrc(c, j, par), ksrc(c, gb),
                        start=(c == 0), stop=(c == HC - 1)),
                        inc="pe" if c == HC - 1 else None)
        if t is not None and oproj:
            pl.wait("tensor", "act", ct_ev)
            pl.wait_many("tensor", bank_free[7])
            pe_o = {}
            for c in range(HC):
                for half in range(2):
                    pe_o[half] = pl.do("tensor", lambda h, c=c, half=half: h.matmul(
                        P.bankf(6 + half), bufs["ctxT"][:, c * 128:(c + 1) * 128],
                        bufs["wob"][:, c * 1024 + half * 512:
                                    c * 1024 + (half + 1) * 512],
                        start=(c == 0), stop=(c == HC - 1)),
                        inc="pe" if c == HC - 1 else None)
        if t is not None and oproj:
            pl.wait_many("vector", [("x%d" % ppar, 0)])
            pl.wait_many("vector", outb_free[ppar])
            dxo = None
            for half in range(2):
                pl.wait("vector", "pe", pe_o[half])
                dxo = pl.do("vector", lambda h, half=half, ppar=ppar:
                            h.tensor_add(
                    outb[ppar][:, half * 512:(half + 1) * 512],
                    P.bankf(6 + half),
                    xbuf[ppar][:, half * 512:(half + 1) * 512]), inc="dve")
                bank_free[6 + half] = [("dve", dxo)]
            xbuf_free[ppar] = [("dve", dxo)]
            pl.wait("gpsimd", "dve", dxo)
            gst = pl.dma("gpsimd", lambda h, j=j, ppar=ppar: h.dma_start(
                out=dst_hbm[(j - 1) * 128:j * 128, :], in_=outb[ppar][:]),
                "so%d" % ppar)
            outb_free[ppar] = [("so%d" % ppar, gst)]
            t["dxo"] = dxo
            t["ppar"] = ppar
        if j < NT:
            for gb in (6, 7):
                pl.wait_many("tensor", bank_free[gb])
                if kvm is not None:
                    pl.waitv("tensor", "pk", kvm[gb])
                for c in range(HC):
                    pe_s[gb] = pl.do("tensor", lambda h, c=c, gb=gb, j=j, par=par:
                                     h.matmul(
                        P.bankf(gb), qsrc(c, j, par), ksrc(c, gb),
                        start=(c == 0), stop=(c == HC - 1)),
                        inc="pe" if c == HC - 1 else None)
            qbuf_free[par] = [("pe", pe_s[7])]
            if t is not None:
                pl.wait("scalar", "pe", pe_tr[3])
            a_exp = {}
            for gb in range(8):
                pl.wait("scalar", "pe", pe_s[gb])
                a_exp[gb] = pl.do("scalar", lambda h, gb=gb: h.activation(
                    Abuf[:, gb * 512:(gb + 1) * 512], P.bankf(gb), AF.Exp,
                    bias=cbias[:, 0:1], scale=scale,
                    accum_out=zp[:, gb:gb + 1]), inc="act")
            st8[j] = {"exp": a_exp}
        if T is not None and t is not None:
            trail_A(j - 1, t["ppar"], t["dxo"])

    if T is not None:
        trail_B(NT - 2, None)
        trail_B(NT - 1, None)
        trail_fin()


def emit_moe(P, pl, sh, mg, src_hbm, dst_hbm, wtb, stage=None):
    xbuf, sq, xn, stats = sh["xbuf"], sh["sq"], sh["xn"], sh["stats"]
    xnT_t, wgb, sel_s, cnts_s = sh["xnT_t"], sh["wgb"], sh["sel_s"], sh["cnts_s"]
    tri_s, identf_s = sh["tri_s"], sh["identf_s"]
    onesrow_s, onescol_s, identb_s = sh["onesrow_s"], sh["onescol_s"], sh["identb_s"]
    gat = dict(sh["gat"])
    gat.update(mg["gat_x"])
    acc, xob, gt_s, runbase, b_s = mg["acc"], mg["xob"], mg["gt_s"], \
        mg["runbase"], mg["b_s"]
    lg = P.bankf(1, 512)[:, 0:NE]

    xb4 = xbuf + mg["xb2"]
    xn4 = xn + mg["xn2"]
    st4 = stats + mg["st2"]
    xnT4 = xnT_t + mg["xnT2"]
    bank_free = {g: [("dve", pl.cnt["dve"])] for g in range(8)}
    xbuf_free = [[("dve", pl.cnt["dve"])]] * 4
    xn_free = [[("pe", pl.cnt["pe"])]] * 4
    xnTt_free = [[("pe", pl.cnt["pe"])]] * 4
    acc_free = [[("dve", pl.cnt["dve"])]] * 2
    xob_free = [[("so0", 0)], [("so1", 0)]]
    prev_act = [pl.cnt["act"]] * 4
    rb = {"pe": 0, "dve": 0}
    st_stage = {"prev": [pl.cnt["act"]] * 2,
                "xnfree": [[("pe", 0)], [("pe", 0)]],
                "tfree": [[("pe", 0)], [("pe", 0)]]}
    st_prep = {}
    st_ln = {}
    st_dev = {}
    st_gate = {}
    abias_of = {}

    def prep_ln(j):
        par = j % 4
        pl.wait_many("sync", xbuf_free[par])
        sld = pl.dma("sync", lambda h, j=j, par=par: h.dma_start(
            out=xb4[par][:], in_=src_hbm[j * 128:(j + 1) * 128, :]), "x%d" % par)
        xn_evt, xfree, pa = emit_ln(P, pl, xb4[par][:], xn4[par][:], sq[:],
                                    st4[par], [("x%d" % par, sld)] + xn_free[par],
                                    prev_act[par])
        prev_act[par] = pa
        xbuf_free[par] = list(xfree)
        st_ln[j] = xn_evt

    def prep_tr(j):
        par = j % 4
        pe_t = emit_transpose8(P, pl, lambda c, par=par: xn4[par][:, c * 128:(c + 1) * 128],
                               0, st_ln[j], bank_free[0] + xnTt_free[par],
                               identb_s)
        xn_free[par] = [("pe", pe_t)]
        pl.wait("vector", "pe", pe_t)
        dev = pl.do("vector", lambda h, par=par: h.tensor_copy(
            xnT4[par][:], P.bankb(0)), inc="dve")
        bank_free[0] = [("dve", dev)]
        st_dev[j] = dev

    def prep_lg(j):
        par = j % 4
        pl.wait("tensor", "dve", max(st_dev[j], pl.g_lastoh))
        pl.wait("tensor", "act", pl.g_laste8)
        pe_lg = None
        for c in range(HC):
            pe_lg = pl.do("tensor", lambda h, c=c, par=par: h.matmul(
                lg, xnT4[par][:, c * 128:(c + 1) * 128], wgb[:, c * NE:(c + 1) * NE],
                start=(c == 0), stop=(c == HC - 1)),
                inc="pe" if c == HC - 1 else None)
        st_prep[j] = pe_lg

    def gate(i):
        par = i % 2
        G = gat["G"][par]
        pl.wait("vector", "pe", st_prep[i])
        d = pl.do("vector", lambda h: h.reduce_max(gat["mx8"][:], lg, axis=AX.X),
                  inc="dve")
        pl.wait("vector", "dve", d)
        doh = pl.do("vector", lambda h: h.tensor_scalar(
            gat["oh"][:], lg, gat["mx8"][:, 0:1], 0.0, op0=ALU.subtract,
            op1=ALU.is_ge), inc="dve")
        pl.g_lastoh = doh
        dn = pl.do("vector", lambda h: h.tensor_scalar(
            gat["nmx"][:], gat["mx8"][:], -1.0, 0.0, op0=ALU.mult, op1=ALU.add),
            inc="dve")
        pl.wait("scalar", "dve", dn)
        a8 = pl.do("scalar", lambda h: h.activation(
            gat["e8"][:], lg, AF.Exp, bias=gat["nmx"][:, 0:1], scale=1.0,
            accum_out=gat["z8"][:]), inc="act")
        pl.g_laste8 = a8
        pl.wait("vector", "act", a8)
        pl.do("vector", lambda h: h.reciprocal(gat["gate0"][:], gat["z8"][:]))
        pl.wait_many("tensor", bank_free[2])
        pl.wait("tensor", "dve", max(doh, rb["dve"]))
        pl.do("tensor", lambda h: h.matmul(
            P.bankf(2, 512)[:, 0:NE], tri_s[:], gat["oh"][:], start=True, stop=False))
        peL = pl.do("tensor", lambda h: h.matmul(
            P.bankf(2, 512)[:, 0:NE], onesrow_s[:], runbase[:], start=False,
            stop=True), inc="pe")
        rb["pe"] = peL
        pl.wait_many("tensor", bank_free[3])
        pecs = pl.do("tensor", lambda h: h.matmul(
            P.bankf(3, 512)[0:1, 0:NE], onescol_s[:], gat["oh"][:],
            start=True, stop=True), inc="pe")
        pl.wait("vector", "pe", peL)
        pl.do("vector", lambda h: h.tensor_mul(
            gat["tmp8"][:], P.bankf(2, 512)[:, 0:NE], gat["oh"][:]))
        dls = pl.do("vector", lambda h: h.reduce_sum(
            gat["Ls"][:], gat["tmp8"][:], axis=AX.X), inc="dve")
        pl.do("vector", lambda h: h.tensor_scalar(
            gat["keep"][:], gat["Ls"][:], CAP, 1.0, op0=ALU.is_lt, op1=ALU.mult))
        dgg = pl.do("vector", lambda h: h.tensor_mul(
            gat["gg"][:], gat["keep"][:], gat["gate0"][:]), inc="dve")
        pl.wait("vector", "dve", dgg)
        dG = pl.do("vector", lambda h, G=G: h.tensor_scalar(
            G[:], gat["oh"][:], gat["gg"][:, 0:1], 0.0, op0=ALU.mult,
            op1=ALU.add), inc="dve")
        pl.wait("vector", "pe", max(pecs, rb["pe"]))
        rb["dve"] = pl.do("vector", lambda h: h.tensor_add(
            runbase[:], runbase[:], P.bankf(3, 512)[0:1, 0:NE]), inc="dve")
        st_gate[i] = (dG, dls, rb["dve"], par, G)

    def gate_b(i):
        dG, dls, rbdve, par, G = st_gate[i]
        pl.wait("tensor", "dve", max(dG, rbdve, dls))
        pegt = pl.do("tensor", lambda h, G=G: h.transpose(
            P.ps[0:8, 3 * 512:3 * 512 + 128], G[:], identf_s[:]), inc="pe")
        pl.wait("vector", "pe", pegt)
        dgt = pl.do("vector", lambda h: h.tensor_copy(
            gt_s[:], P.ps[0:8, 3 * 512:3 * 512 + 128]), inc="dve")
        pl.wait("tensor", "dve", dgt)
        pebias = {}
        for half in range(2):
            pebias[half] = pl.do("tensor", lambda h, half=half: h.matmul(
                P.bankf(2 + half), gt_s[:], b_s[0:NE, half * 512:(half + 1) * 512],
                start=True, stop=True), inc="pe")
        pl.wait_many("scalar", acc_free[par])
        abias = None
        for half in range(2):
            pl.wait("scalar", "pe", pebias[half])
            abias = pl.do("scalar", lambda h, half=half, par=par: h.activation(
                acc[par][:, half * 512:(half + 1) * 512], P.bankf(2 + half),
                AF.Identity), inc="act")
        bank_free[2] = [("act", abias), ("dve", dls)]
        bank_free[3] = [("act", abias)]
        abias_of[i] = abias

    prep_ln(0)
    prep_tr(0)
    prep_lg(0)
    prep_ln(1)
    prep_tr(1)
    pl.wait("tensor", "ccr")
    pl.wait("tensor", "c")
    pl.wait("tensor", "dve", pl.cnt["dve"])
    perb = pl.do("tensor", lambda h: h.matmul(
        P.bankf(2, 512)[0:1, 0:NE], sel_s[:], cnts_s[:], start=True, stop=True),
        inc="pe")
    pl.wait("vector", "pe", perb)
    drb = pl.do("vector", lambda h: h.tensor_copy(
        runbase[:], P.bankf(2, 512)[0:1, 0:NE]), inc="dve")
    rb["pe"] = perb
    rb["dve"] = drb
    gate(0)
    gate_b(0)
    prep_lg(1)
    for i in range(NT):
        par = i % 2
        par4 = i % 4
        ebanks = [(4, 5), (6, 7)]
        pe_e = {}
        dacc = None
        abias = abias_of.pop(i)
        for e in range(NE):
            ba, bb = ebanks[e % 2]
            pl.wait("tensor", "wt%d" % e)
            for bk in (ba, bb):
                pl.wait_many("tensor", bank_free[bk])
            for c in range(HC):
                for half, bk in ((0, ba), (1, bb)):
                    pe_e[(e, half)] = pl.do(
                        "tensor", lambda h, c=c, e=e, bk=bk, half=half, par4=par4:
                        h.matmul(
                            P.bankf(bk), xnT4[par4][:, c * 128:(c + 1) * 128],
                            wtb[:, (e * 8 + c) * 1024 + half * 512:
                                (e * 8 + c) * 1024 + (half + 1) * 512],
                            start=(c == 0), stop=(c == HC - 1)),
                        inc="pe" if c == HC - 1 else None)
            if e == 1 and i + 2 < NT:
                prep_ln(i + 2)
            if e == 3 and i + 1 < NT:
                if i + 2 < NT:
                    prep_tr(i + 2)
            if e == 4 and i + 1 < NT:
                gate(i + 1)
            if e == 5 and i + 2 < NT:
                prep_lg(i + 2)
            if e == 6 and i + 1 < NT:
                gate_b(i + 1)
            if e == 0:
                pl.wait("vector", "act", abias)
            for half, bk in ((0, ba), (1, bb)):
                pl.wait("vector", "pe", pe_e[(e, half)])
                dacc = pl.do("vector", lambda h, half=half, bk=bk, e=e, par=par:
                             h.scalar_tensor_tensor(
                    acc[par][:, half * 512:(half + 1) * 512], P.bankf(bk),
                    gat["G"][par][:, e:e + 1],
                    acc[par][:, half * 512:(half + 1) * 512],
                    op0=ALU.mult, op1=ALU.add), inc="dve")
                bank_free[bk] = [("dve", dacc)]
        xnTt_free[par4] = [("pe", pl.cnt["pe"])]
        pl.wait_many("vector", xob_free[par])
        dxo = pl.do("vector", lambda h, par=par, par4=par4: h.tensor_add(
            xob[par][:], acc[par][:], xb4[par4][:]), inc="dve")
        acc_free[par] = [("dve", dxo)]
        xbuf_free[par4] = xbuf_free[par4] + [("dve", dxo)]
        pl.wait("gpsimd", "dve", dxo)
        gst = pl.dma("gpsimd", lambda h, i=i, par=par: h.dma_start(
            out=dst_hbm[i * 128:(i + 1) * 128, :], in_=xob[par][:]), "so%d" % par)
        xob_free[par] = [("so%d" % par, gst)]


def alloc_moe_group(P, wt_name):
    mg = {}
    mg["wtb"] = P.sb(wt_name, [128, NE * HC * H], BF16)
    mg["acc"] = [P.sb(f"{wt_name}_acc{p}", [128, H], F32) for p in range(2)]
    mg["xob"] = [P.sb(f"{wt_name}_xob{p}", [128, H], F32) for p in range(2)]
    mg["b_s"] = P.sb(f"{wt_name}_b_s", [NE, H], F32)
    mg["gt_s"] = P.sb(f"{wt_name}_gt", [NE, 128], F32)
    mg["runbase"] = P.sb(f"{wt_name}_rb", [1, NE], F32)
    mg["xb2"] = [P.sb(f"{wt_name}_xb{p}", [128, H], F32) for p in range(2)]
    mg["xn2"] = [P.sb(f"{wt_name}_xn{p}", [128, H], BF16) for p in range(2)]
    mg["xnT2"] = [P.sb(f"{wt_name}_xnT{p}", [128, H], BF16) for p in range(2)]
    mg["st2"] = [{k: P.sb(f"{wt_name}_s{k}{p}", [128, 1], F32)
                  for k in ("mraw", "mean", "msq", "var", "lnv", "rstd",
                            "sums", "nb2")} for p in range(2)]
    mg["sxn"] = [P.sb(f"{wt_name}_sxn{p}", [128, H], BF16) for p in range(2)]
    mg["sxnT"] = [P.sb(f"{wt_name}_sxT{p}", [128, H], BF16) for p in range(2)]
    mg["sst"] = [{k: P.sb(f"{wt_name}_t{k}{p}", [128, 1], F32)
                  for k in ("mraw", "mean", "msq", "var", "lnv", "rstd",
                            "sums", "nb2")} for p in range(2)]
    gx = {}
    for k, s in (("tmp8", NE), ("Ls", 1), ("keep", 1), ("gg", 1)):
        gx[k] = P.sb(f"{wt_name}_g{k}", [128, s], F32)
    gx["G"] = [P.sb(f"{wt_name}_gG{p}", [128, NE], F32) for p in range(2)]
    mg["gat_x"] = gx
    return mg




def build_fused():
    P = Prog()
    nc = P.nc
    x_in = nc.dram_tensor("x", [SEQ, H], F32, kind="ExternalInput").ap()
    wq_in = nc.dram_tensor("wq", [128, 8192], F32, kind="ExternalInput").ap()
    wk_in = nc.dram_tensor("wk", [128, 8192], F32, kind="ExternalInput").ap()
    wv_in = nc.dram_tensor("wv", [128, 8192], F32, kind="ExternalInput").ap()
    wo_in = nc.dram_tensor("wo", [128, 8192], F32, kind="ExternalInput").ap()
    wg_in = nc.dram_tensor("wg", [128, HC * NE], F32, kind="ExternalInput").ap()
    wt_in = nc.dram_tensor("wt", [128, NE * HC * H], F32, kind="ExternalInput").ap()
    b_in = nc.dram_tensor("b", [NE, H], F32, kind="ExternalInput").ap()
    sel_in = nc.dram_tensor("sel", [NE, 1], F32, kind="ExternalInput").ap()
    myrow_in = nc.dram_tensor("myrow", [1, NE], F32, kind="ExternalInput").ap()
    y_out = nc.dram_tensor("y", [SEQ, H], F32, kind="ExternalOutput").ap()

    qt_hbm = nc.dram_tensor("qt_hbm", [NT, 128, 8, 128], BF16, kind="Internal").ap()
    xn2s_hbm = nc.dram_tensor("xn2s_hbm", [128, NT * H], BF16, kind="Internal").ap()
    xnT2s_hbm = nc.dram_tensor("xnT2s_hbm", [128, HC * SEQ], BF16,
                               kind="Internal").ap()
    x1_hbm = nc.dram_tensor("x1_hbm", [SEQ, H], F32, kind="Internal").ap()
    x2_hbm = nc.dram_tensor("x2_hbm", [SEQ, H], F32, kind="Internal").ap()
    x3_hbm = nc.dram_tensor("x3_hbm", [SEQ, H], F32, kind="Internal").ap()
    cc1i = nc.dram_tensor("cc1i", [NE, NE], F32)
    cc1o = nc.dram_tensor("cc1o", [NE, NE], F32)
    cc2i = nc.dram_tensor("cc2i", [NE, NE], F32)
    cc2o = nc.dram_tensor("cc2o", [NE, NE], F32)

    pl = Pl(nc)

    xbuf = [P.sb(f"xbuf{p}", [128, H], F32) for p in range(2)]
    sq = P.sb("sq", [128, H], BF16)
    xn = [P.sb(f"xn{p}", [128, H], BF16) for p in range(2)]
    stats = [{k: P.sb(f"st_{k}{p}", [128, 1], F32)
              for k in ("mraw", "mean", "msq", "var", "lnv", "rstd", "sums", "nb2")}
             for p in range(2)]
    onescol_s = P.sb("onescol", [128, 1], F32)
    onesrow_s = P.sb("onesrow", [1, 128], F32)
    identb_s = P.sb("identb_s", [128, 128], BF16)
    identf_s = P.sb("identf_s", [128, 128], F32)
    tri_s = P.sb("tri_s", [128, 128], F32)
    wgb = P.sb("wgb", [128, HC * NE], BF16)
    sel_s = P.sb("sel_s", [NE, 1], F32)
    myrowT_s = P.sb("myrowT_s", [1, NE], F32)
    cnts_s = P.sb("cnts_s", [NE, NE], F32)
    cnt_s = P.sb("cnt_s", [1, NE], F32)
    tbl_s = P.sb("tbl_s", [NE, NE], F32)
    xnT_t = [P.sb(f"xnTt{p}", [128, H], BF16) for p in range(2)]
    gat = {}
    for k, s in (("mx8", 1), ("oh", NE), ("nmx", 1), ("e8", NE), ("z8", 1),
                 ("gate0", 1)):
        gat[k] = P.sb("g_" + k, [128, s], F32)
    gat["onescol"] = onescol_s
    sh = {"xbuf": xbuf, "sq": sq, "xn": xn, "stats": stats, "xnT_t": xnT_t,
          "wgb": wgb, "sel_s": sel_s, "cnts_s": cnts_s, "tri_s": tri_s,
          "identf_s": identf_s, "onesrow_s": onesrow_s, "onescol_s": onescol_s,
          "identb_s": identb_s, "gat": gat}
    trail = {"xn": xn, "sq": sq, "stats": stats, "xnT": xnT_t, "wgb": wgb,
             "onescol": onescol_s, "cnt_s": cnt_s,
             "lgs": P.sb("tr_lgs", [128, NE], F32),
             "tmx": P.sb("tr_tmx", [128, 1], F32),
             "ohsum": P.sb("tr_os", [128, NE], F32),
             "ohb": [P.sb(f"tr_oh{p}", [128, NE], F32) for p in range(2)]}

    for buf, src in ((identb_s, P.identb.ap()), (identf_s, P.identf.ap()),
                     (tri_s, P.c_triu.ap()), (onescol_s, P.c_onescol.ap()),
                     (onesrow_s, P.c_onesrow.ap()), (sel_s, sel_in),
                     (myrowT_s, myrow_in)):
        pl.dma("sync", lambda h, buf=buf, src=src: h.dma_start(out=buf[:], in_=src),
               "c")
    pl.dma("gpsimd", lambda h: h.dma_start(out=wgb[:], in_=wg_in), "wgl")
    pl.wait("tensor", "c")

    markA = len(P.guards)
    KT = P.sb("KT", [128, HC * SEQ], BF16)
    V = P.sb("Vb", [128, NT * H], BF16)
    markA1 = len(P.guards)
    wvb = P.sb("wvb", [128, 8192], BF16)
    wring = [P.sb(f"wring{p}", [128, 1024], BF16) for p in range(4)]
    xnTg = [P.sb(f"xnTg{p}", [128, HC * 512], BF16) for p in range(2)]
    qstage = P.sb("qstage", [128, 8 * 512], BF16)

    pl.dma("gpsimd", lambda h: h.dma_start(out=wvb[:], in_=wv_in), "wv")

    bank_free = {g: [("dve", 0)] for g in range(8)}
    xbuf_free = [[("dve", 0)]] * 2
    xn_free = [[("pe", 0)]] * 2
    xnTg_free = [("pe", 0), ("pe", 0)]
    qstage_free = ("qs", 0)
    ring_free = [("pe", 0)] * 4
    prev_act = [0, 0]

    for i in range(NT):
        par = i % 2
        g, m = i // 4, i % 4
        pl.wait_many("sync", xbuf_free[par])
        sld = pl.dma("sync", lambda h, i=i, par=par: h.dma_start(
            out=xbuf[par][:], in_=x_in[i * 128:(i + 1) * 128, :]), "x%d" % par)
        xn_evt, xfree, pa = emit_ln(P, pl, xbuf[par][:], xn[par][:], sq[:], stats[par],
                                    [("x%d" % par, sld)] + xn_free[par], prev_act[par])
        prev_act[par] = pa
        xbuf_free[par] = list(xfree)
        gpar = g % 2
        extra = [xnTg_free[gpar]] if m == 0 else []
        pe_t = emit_transpose8(P, pl, lambda c, par=par: xn[par][:, c * 128:(c + 1) * 128],
                               6, xn_evt, bank_free[6] + extra, identb_s)
        xn_free[par] = [("pe", pe_t)]
        pl.wait("vector", "pe", pe_t)
        dst = xnTg[gpar][:].rearrange("p (c t) -> p c t", c=HC)[:, :, m * 128:(m + 1) * 128]
        src = P.bankb(6)[:].rearrange("p (c q) -> p c q", c=HC)
        dev = pl.do("vector", lambda h, dst=dst, src=src: h.tensor_copy(dst, src),
                    inc="dve")
        bank_free[6] = [("dve", dev)]
        pl.wait("tensor", "dve", dev)
        if i == 0:
            pl.wait("tensor", "wv")
        for bk in (0, 1):
            pl.wait_many("tensor", bank_free[bk])
        pev = {}
        for half in range(2):
            for c in range(HC):
                pev[half] = pl.do("tensor", lambda h, half=half, c=c, m=m, gpar=gpar:
                                  h.matmul(
                    P.bankf(half),
                    xnTg[gpar][:, c * 512 + m * 128: c * 512 + (m + 1) * 128],
                    wvb[:, c * 1024 + half * 512: c * 1024 + (half + 1) * 512],
                    start=(c == 0), stop=(c == HC - 1)),
                    inc="pe" if c == HC - 1 else None)
        for half in range(2):
            pl.wait("vector", "pe", pev[half])
            dv = pl.do("vector", lambda h, half=half, i=i: h.tensor_copy(
                V[:, i * 1024 + half * 512: i * 1024 + (half + 1) * 512],
                P.bankf(half)), inc="dve")
            bank_free[half] = [("dve", dv)]

        if m == 3:
            seq16 = [("q", wq_in, s) for s in range(8)] + \
                    [("k", wk_in, s) for s in range(8)]

            def wload(t):
                which, win, s = seq16[t]
                slot = t % 4
                pl.wait_many("gpsimd", [ring_free[slot]])
                pl.dma("gpsimd", lambda h, win=win, s=s, slot=slot: h.dma_start(
                    out=wring[slot][:], in_=win[:, s * 1024:(s + 1) * 1024]),
                    "wr%d" % slot)

            for t in range(4):
                wload(t)
            for t in range(16):
                which, win, s = seq16[t]
                slot = t % 4
                pl.wait("tensor", "wr%d" % slot)
                pl.wait_many("tensor", bank_free[s])
                if which == "q" and s == 0:
                    pl.wait_many("tensor", [qstage_free])
                pes = None
                for c in range(HC):
                    pes = pl.do("tensor", lambda h, s=s, c=c, slot=slot, gpar=gpar:
                                h.matmul(
                        P.bankf(s), wring[slot][:, c * 128:(c + 1) * 128],
                        xnTg[gpar][:, c * 512:(c + 1) * 512],
                        start=(c == 0), stop=(c == HC - 1)),
                        inc="pe" if c == HC - 1 else None)
                ring_free[slot] = ("pe", pes)
                if t + 4 < 16:
                    wload(t + 4)
                pl.wait("vector", "pe", pes)
                if which == "q":
                    pl.wait_many("vector", [qstage_free])
                    dv = pl.do("vector", lambda h, s=s: h.tensor_copy(
                        qstage[:, s * 512:(s + 1) * 512], P.bankf(s)), inc="dve")
                else:
                    dv = pl.do("vector", lambda h, s=s, g=g: h.tensor_copy(
                        KT[:, s * SEQ + g * 512: s * SEQ + (g + 1) * 512],
                        P.bankf(s)), inc="dve")
                bank_free[s] = [("dve", dv)]
                if which == "q" and s == 7:
                    pl.wait("gpsimd", "dve", pl.cnt["dve"])
                    gpv = None
                    for mm in range(4):
                        i2 = 4 * g + mm
                        srcq = qstage[:].rearrange("p (s q) -> p s q", s=8)[
                            :, :, mm * 128:(mm + 1) * 128]
                        gpv = pl.dma("gpsimd", lambda h, i2=i2, srcq=srcq: h.dma_start(
                            out=qt_hbm[i2], in_=srcq), "qs")
                    qstage_free = ("qs", gpv)
            xnTg_free[gpar] = ("pe", pl.cnt["pe"])

    pl.join()
    P.free_to(markA1)

    wob = P.sb("wob", [128, 8192], BF16)
    qbuf = [P.sb(f"qbuf{p}", [128, 1024], BF16) for p in range(2)]
    bufs = {
        "Abuf": P.sb("Abuf", [128, SEQ], BF16),
        "ATb": P.sb("ATb", [128, SEQ], BF16),
        "ctxb": P.sb("ctxb", [128, H], BF16),
        "ctxT": P.sb("ctxT", [128, H], BF16),
        "cbias": P.sb("cbias", [128, 1], F32),
        "zp": P.sb("zp", [128, 8], F32),
        "sm": {k: P.sb("sm_" + k, [128, 1], F32) for k in ("zs", "rz")},
        "xbuf": xbuf,
        "outb": [P.sb(f"x1b{p}", [128, H], F32) for p in range(2)],
        "wob": wob,
    }
    pl.dma("gpsimd", lambda h: h.dma_start(out=wob[:], in_=wo_in), "wo")
    pl.dma("sync", lambda h: h.dma_start(out=bufs["cbias"][:], in_=P.c_zero1.ap()), "c")
    pl.wait("tensor", "wo")
    pl.wait("scalar", "c")

    def qload(pl, j, par):
        pl.dma("sync", lambda h, j=j, par=par: h.dma_start(
            out=qbuf[par][:], in_=qt_hbm[j].rearrange("p s q -> p (s q)")),
            "q%d" % par)
        return "q%d" % par

    bufs["qload"] = qload
    emit_attention(
        P, pl, bufs,
        qsrc=lambda c, j, par: qbuf[par][:, c * 128:(c + 1) * 128],
        ksrc=lambda c, gb: KT[:, c * SEQ + gb * 512: c * SEQ + (gb + 1) * 512],
        vsrc=lambda kc, half: V[:, kc * 1024 + half * 512: kc * 1024 + (half + 1) * 512],
        xres_hbm=x_in, dst_hbm=x1_hbm, scale=1.0 / 32.0, oproj=True,
        identb_s=identb_s, trail=trail)

    pl.join()
    P.free_to(markA)

    markM = len(P.guards)
    mg1 = alloc_moe_group(P, "wt1")
    for e in range(NE):
        pl.dma("gpsimd", lambda h, e=e: h.dma_start(
            out=mg1["wtb"][:, e * 8192:(e + 1) * 8192],
            in_=wt_in[:, e * 8192:(e + 1) * 8192]), "wt%d" % e)
    pl.dma("sync", lambda h: h.dma_start(out=mg1["b_s"][:], in_=b_in), "c")
    emit_collective(P, pl, myrowT_s, cnt_s, tbl_s, cc1i, cc1o, cnts_s)

    emit_moe(P, pl, sh, mg1, x1_hbm, x2_hbm, mg1["wtb"])
    pl.join()
    P.free_to(markM)

    markB = len(P.guards)
    xn2 = P.sb("xn2", [128, NT * H], BF16)
    xnT2 = P.sb("xnT2", [128, HC * SEQ], BF16)
    bufsB = {
        "Abuf": P.sb("Abuf2", [128, SEQ], BF16),
        "ATb": P.sb("ATb2", [128, SEQ], BF16),
        "cbias": P.sb("cbias2", [128, 1], F32),
        "zp": P.sb("zp2", [128, 8], F32),
        "sm": {k: P.sb("sm2_" + k, [128, 1], F32) for k in ("zs", "rz")},
        "xbuf": xbuf,
        "outb": [P.sb(f"x3b{p}", [128, H], F32) for p in range(2)],
    }
    pl.dma("sync", lambda h: h.dma_start(out=bufsB["cbias"][:], in_=P.c_m1024.ap()),
           "c")
    pl.wait("scalar", "c")
    xb4 = xbuf + [P.sb(f"xbufB{p}", [128, H], F32) for p in range(2)]
    st4 = stats + [{k: P.sb(f"stB_{k}{p}", [128, 1], F32)
                    for k in ("mraw", "mean", "msq", "var", "lnv", "rstd",
                              "sums", "nb2")} for p in range(2)]
    prev_act = [pl.cnt["act"]] * 4
    xbuf_free = [[("dve", pl.cnt["dve"])]] * 4
    bk_free = [[("dve", pl.cnt["dve"])]] * 2
    for i in range(NT):
        par = i % 4
        bk = i % 2
        pl.wait_many("sync", xbuf_free[par])
        sld = pl.dma("sync", lambda h, i=i, par=par: h.dma_start(
            out=xb4[par][:], in_=x2_hbm[i * 128:(i + 1) * 128, :]), "x%d" % par)
        xn_evt, xfree, pa = emit_ln(P, pl, xb4[par][:], xn2[:, i * 1024:(i + 1) * 1024],
                                    sq[:], st4[par], [("x%d" % par, sld)],
                                    prev_act[par])
        prev_act[par] = pa
        xbuf_free[par] = list(xfree)
        pe_t = emit_transpose8(
            P, pl,
            lambda c, i=i: xn2[:, i * 1024 + c * 128: i * 1024 + (c + 1) * 128],
            bk, xn_evt, bk_free[bk], identb_s)
        pl.wait("vector", "pe", pe_t)
        dst = xnT2[:].rearrange("p (c t) -> p c t", c=HC)[:, :, i * 128:(i + 1) * 128]
        src = P.bankb(bk)[:].rearrange("p (c q) -> p c q", c=HC)
        dev = pl.do("vector", lambda h, dst=dst, src=src: h.tensor_copy(dst, src),
                    inc="dve")
        bk_free[bk] = [("dve", dev)]
    pl.join()

    emit_attention(
        P, pl, bufsB,
        qsrc=lambda c, j, par: xnT2[:, c * SEQ + j * 128: c * SEQ + (j + 1) * 128],
        ksrc=lambda c, gb: xnT2[:, c * SEQ + gb * 512: c * SEQ + (gb + 1) * 512],
        vsrc=lambda kc, half: xn2[:, kc * 1024 + half * 512: kc * 1024 + (half + 1) * 512],
        xres_hbm=x2_hbm, dst_hbm=x3_hbm, scale=1.0, oproj=False,
        identb_s=identb_s, trail=trail)

    pl.join()
    P.free_to(markB)

    markM2 = len(P.guards)
    mg2 = alloc_moe_group(P, "wt2")
    for e in range(NE):
        pl.dma("gpsimd", lambda h, e=e: h.dma_start(
            out=mg2["wtb"][:, e * 8192:(e + 1) * 8192],
            in_=wt_in[:, e * 8192:(e + 1) * 8192]), "wt%d" % e)
    pl.dma("sync", lambda h: h.dma_start(out=mg2["b_s"][:], in_=b_in), "c")
    emit_collective(P, pl, myrowT_s, cnt_s, tbl_s, cc2i, cc2o, cnts_s)

    emit_moe(P, pl, sh, mg2, x3_hbm, y_out, mg2["wtb"])

    for strm in ("so0", "so1"):
        pl.wait("gpsimd", strm)
        pl.wait("sync", strm)

    block = nc.Block()
    with block:
        pl.play(block)
    return nc



_CACHE = {}


def _get_prog():
    if "p" not in _CACHE:
        _CACHE["p"] = build_fused()
    return _CACHE["p"]


def kernel(L0_attn_a_x, L0_attn_a_W_qkv, L0_attn_a_W_out, L0_ffn_wg, L0_ffn_W,
           L0_ffn_b):
    x = np.ascontiguousarray(np.asarray(L0_attn_a_x, dtype=np.float32))
    Wqkv = np.asarray(L0_attn_a_W_qkv, dtype=np.float32)
    Wq, Wk, Wv = Wqkv[0::3], Wqkv[1::3], Wqkv[2::3]
    st = stage_weights(Wq, Wk, Wv, np.asarray(L0_attn_a_W_out, np.float32),
                       np.asarray(L0_ffn_wg, np.float32),
                       np.asarray(L0_ffn_W, np.float32),
                       np.asarray(L0_ffn_b, np.float32))
    cores = list(range(8))
    prog = _get_prog()

    ins = []
    for c in cores:
        sel = np.ascontiguousarray(
            (np.arange(NE) < c).astype(np.float32).reshape(NE, 1))
        myrow = np.zeros((1, NE), np.float32)
        myrow[0, c] = 1.0
        ins.append({"x": np.ascontiguousarray(x[c]), "wq": st["wq"],
                    "wk": st["wk"], "wv": st["wv"], "wo": st["wo"],
                    "wg": st["wg"], "wt": st["wt"], "b": st["b"],
                    "sel": sel, "myrow": myrow})
    r = run_bass_kernel_spmd(prog, ins, core_ids=cores)
    out = np.stack([r.results[c]["y"] for c in cores]).astype(np.float32)
    return out


# version 23 (final)
# speedup vs baseline: 1.0187x; 1.0077x over previous
import numpy as np
import ml_dtypes
import concourse.bass as bass
import concourse.mybir as mybir
from concourse.bass_utils import run_bass_kernel_spmd

F32 = mybir.dt.float32
BF16 = mybir.dt.bfloat16
AF = mybir.ActivationFunctionType
ALU = mybir.AluOpType
AX = mybir.AxisListType
BF = ml_dtypes.bfloat16

SEQ = 4096
H = 1024
NT = SEQ // 128
HC = H // 128
NE = 8
CAP = float(SEQ * 8 // NE)
EPS = 1e-5
ENGINES = ("sync", "gpsimd", "tensor", "vector", "scalar")


class Pl:

    ENG_SEMS = ("pe", "dve", "act")

    def __init__(self, nc):
        self.nc = nc
        self.q = {e: [] for e in ENGINES}
        self.sems = {n: nc.alloc_semaphore("s_" + n) for n in self.ENG_SEMS}
        self.cnt = {n: 0 for n in self.ENG_SEMS}
        self.lastw = {}
        self.g_lastoh = 0
        self.g_laste8 = 0

    def _stream(self, name):
        if name not in self.sems:
            self.sems[name] = self.nc.alloc_semaphore("s_" + name)
            self.cnt[name] = 0
        return self.sems[name]

    def dma(self, eng, fn, stream):
        s = self._stream(stream)
        self.cnt[stream] += 16
        self.q[eng].append(lambda h, fn=fn, s=s: fn(h).then_inc(s, 16))
        return self.cnt[stream]

    def cc(self, fn, stream):
        s = self._stream(stream)
        self.cnt[stream] += 1
        self.q["gpsimd"].append(lambda h, fn=fn, s=s: fn(h).then_inc(s, 1))
        return self.cnt[stream]

    def wait(self, eng, name, val=None):
        if name not in self.ENG_SEMS:
            self._stream(name)
            val = None
        v = self.cnt[name] if val is None else val
        if v <= 0:
            return
        key = (eng, name)
        if self.lastw.get(key, -1) >= v:
            return
        self.lastw[key] = v
        s = self.sems[name]
        self.q[eng].append(lambda h, s=s, v=v: h.wait_ge(s, v))

    def wait_many(self, eng, pairs):
        for nm, v in pairs:
            self.wait(eng, nm, v)

    def waitv(self, eng, name, v):
        if v <= 0:
            return
        key = (eng, name)
        if self.lastw.get(key, -1) >= v:
            return
        self.lastw[key] = v
        s = self._stream(name)
        self.q[eng].append(lambda h, s=s, v=v: h.wait_ge(s, v))

    AUTO = {"vector": "dve", "scalar": "act"}

    def do(self, eng, fn, inc=None):
        auto = self.AUTO.get(eng)
        if auto is not None:
            self.wait(eng, auto, self.cnt[auto])
            if inc is None:
                inc = auto
            assert inc == auto
        if inc is None:
            self.q[eng].append(fn)
            return None
        s = self.sems[inc]
        self.cnt[inc] += 1
        self.q[eng].append(lambda h, fn=fn, s=s: fn(h).then_inc(s, 1))
        return self.cnt[inc]

    def join(self, engs=ENGINES, skip=()):
        snap = dict(self.cnt)
        for e in engs:
            for n, v in snap.items():
                if n in skip:
                    continue
                self.wait(e, n, v)

    def play(self, block):
        for eng in ENGINES:
            ops = self.q[eng]
            if not ops:
                continue
            getattr(block, eng)(lambda h, ops=ops: [f(h) for f in ops])


class Prog:
    def __init__(self):
        self.nc = bass.Bass("TRN2", target_bir_lowering=False)
        nc = self.nc
        self.ps = nc.alloc_psum_tensor("ps", [128, 4096], F32)
        self.identb = nc.inline_tensor(np.eye(128, dtype=BF), "c_identb")
        self.identf = nc.inline_tensor(np.eye(128, dtype=np.float32), "c_identf")
        self.c_triu = nc.inline_tensor(
            np.triu(np.ones((128, 128), np.float32), 1), "c_triu")
        self.c_onescol = nc.inline_tensor(np.ones((128, 1), np.float32), "c_onescol")
        self.c_onesrow = nc.inline_tensor(np.ones((1, 128), np.float32), "c_onesrow")
        self.c_zero1 = nc.inline_tensor(np.zeros((128, 1), np.float32), "c_zero1")
        self.c_m1024 = nc.inline_tensor(
            np.full((128, 1), -1024.0, np.float32), "c_m1024")
        self.guards = []

    def sb(self, name, shape, dtype):
        g = self.nc.sbuf_tensor(name, shape, dtype)
        t = g.__enter__()
        self.guards.append(g)
        return t

    def free_to(self, mark):
        while len(self.guards) > mark:
            self.guards.pop().__exit__(None, None, None)

    def bankf(self, g, n=512):
        return self.ps[:, g * 512:g * 512 + n]

    def bankb(self, g):
        v = self.ps[:, g * 512:(g + 1) * 512].bitcast(BF16)
        assert tuple(v.shape) == (128, 1024), v.shape
        return v


def emit_ln(P, pl, xsrc, xn_dst, sq, st, wait_src, prev_act):
    pl.wait_many("vector", wait_src)
    pl.wait_many("scalar", wait_src)
    pl.wait("vector", "act", prev_act)
    pl.do("vector", lambda h: h.reduce_sum(st["mraw"][:], xsrc, axis=AX.X))
    pl.do("vector", lambda h: h.scalar_tensor_tensor(
        sq, xsrc, 1.0, xsrc, op0=ALU.mult, op1=ALU.mult, accum_out=st["sums"][:]))
    dm = pl.do("vector", lambda h: h.tensor_scalar(
        st["mean"][:], st["mraw"][:], 1.0 / 1024.0, 0.0, op0=ALU.mult, op1=ALU.add),
        inc="dve")
    pl.wait("vector", "dve", dm)
    pl.do("vector", lambda h: h.tensor_scalar(
        st["msq"][:], st["mean"][:], st["mean"][:, 0:1], -EPS,
        op0=ALU.mult, op1=ALU.add))
    d = pl.do("vector", lambda h: h.scalar_tensor_tensor(
        st["var"][:], st["sums"][:], 1.0 / 1024.0, st["msq"][:],
        op0=ALU.mult, op1=ALU.subtract), inc="dve")
    pl.wait("scalar", "dve", d)
    pl.do("scalar", lambda h: h.activation(st["lnv"][:], st["var"][:], AF.Ln,
                                           scale=1.0))
    a = pl.do("scalar", lambda h: h.activation(st["rstd"][:], st["lnv"][:], AF.Exp,
                                               scale=-0.5), inc="act")
    pl.wait("vector", "act", a)
    dnb = pl.do("vector", lambda h: h.tensor_scalar(
        st["nb2"][:], st["mean"][:], st["rstd"][:, 0:1], -1.0,
        op0=ALU.mult, op1=ALU.mult), inc="dve")
    pl.wait("scalar", "dve", dnb)
    d2 = pl.do("scalar", lambda h: h.activation(
        xn_dst, xsrc, AF.Identity, bias=st["nb2"][:, 0:1],
        scale=st["rstd"][:, 0:1]), inc="act")
    return ("act", d2), [("act", d2), ("dve", dnb)], d2


def emit_transpose8(P, pl, src_cols, bank, wait_src, free_at, identb_s):
    tgt = P.bankb(bank)
    pl.wait_many("tensor", [wait_src])
    pl.wait_many("tensor", free_at)
    pe = None
    for c in range(HC):
        pe = pl.do("tensor", lambda h, c=c: h.transpose(
            tgt[:, c * 128:(c + 1) * 128], src_cols(c), identb_s[:]),
            inc="pe" if c == HC - 1 else None)
    return pe


def emit_tr8_dma(P, pl, eng, src_col, dst_col, stream, waits):
    pl.wait_many(eng, waits)
    mk = None
    for c in range(HC):
        mk = pl.dma(eng, lambda h, c=c: h.dma_start(
            dst_col(c), src_col(c), transpose=True), stream)
    return mk


def emit_gating(P, pl, i, xnT_col, wgb, counts_only, gat, b1_free=()):
    lg = P.bankf(1, 512)[:, 0:NE]
    pl.wait("tensor", "dve", pl.g_lastoh)
    pl.wait("tensor", "act", pl.g_laste8)
    pl.wait_many("tensor", list(b1_free))
    pe = None
    for c in range(HC):
        pe = pl.do("tensor", lambda h, c=c: h.matmul(
            lg, xnT_col(c), wgb[:, c * NE:(c + 1) * NE],
            start=(c == 0), stop=(c == HC - 1)),
            inc="pe" if c == HC - 1 else None)
    pl.wait("vector", "pe", pe)
    d = pl.do("vector", lambda h: h.reduce_max(gat["mx8"][:], lg, axis=AX.X), inc="dve")
    pl.wait("vector", "dve", d)
    doh = pl.do("vector", lambda h: h.tensor_scalar(
        gat["oh"][:], lg, gat["mx8"][:, 0:1], 0.0, op0=ALU.subtract, op1=ALU.is_ge),
        inc="dve")
    pl.g_lastoh = doh
    if not counts_only:
        dn = pl.do("vector", lambda h: h.tensor_scalar(
            gat["nmx"][:], gat["mx8"][:], -1.0, 0.0, op0=ALU.mult, op1=ALU.add),
            inc="dve")
        pl.wait("scalar", "dve", dn)
        a = pl.do("scalar", lambda h: h.activation(
            gat["e8"][:], lg, AF.Exp, bias=gat["nmx"][:, 0:1], scale=1.0,
            accum_out=gat["z8"][:]), inc="act")
        pl.g_laste8 = a
        pl.wait("vector", "act", a)
        pl.do("vector", lambda h: h.reciprocal(gat["gate0"][:], gat["z8"][:]))
    pecnt = None
    if counts_only:
        pl.wait("tensor", "dve", doh)
        pecnt = pl.do("tensor", lambda h, i=i: h.matmul(
            P.bankf(3, 512)[0:1, 0:NE], gat["onescol"][:], gat["oh"][:],
            start=(i == 0), stop=(i == NT - 1)), inc="pe")
    return doh, pecnt


def emit_count_phase(P, pl, src_hbm, xbuf, sq, xn, stats, xnT_t, wgb, gat,
                     cnt_s, identb_s):
    xbuf_free = [[("dve", pl.cnt["dve"])]] * 2
    xn_free = [[("pe", pl.cnt["pe"])]] * 2
    xnTt_free = [[("pe", pl.cnt["pe"])]] * 2
    b0_free = [("dve", pl.cnt["dve"])]
    prev_act = [pl.cnt["act"], pl.cnt["act"]]
    for i in range(NT):
        par = i % 2
        pl.wait_many("sync", xbuf_free[par])
        sld = pl.dma("sync", lambda h, i=i, par=par: h.dma_start(
            out=xbuf[par][:], in_=src_hbm[i * 128:(i + 1) * 128, :]), "x%d" % par)
        xn_evt, xfree, pa = emit_ln(P, pl, xbuf[par][:], xn[par][:], sq[:], stats[par],
                                    [("x%d" % par, sld)] + xn_free[par], prev_act[par])
        prev_act[par] = pa
        xbuf_free[par] = list(xfree)
        pe_t = emit_transpose8(P, pl, lambda c, par=par: xn[par][:, c * 128:(c + 1) * 128],
                               0, xn_evt, b0_free + xnTt_free[par], identb_s)
        xn_free[par] = [("pe", pe_t)]
        pl.wait("vector", "pe", pe_t)
        dev = pl.do("vector", lambda h, par=par: h.tensor_copy(
            xnT_t[par][:], P.bankb(0)), inc="dve")
        b0_free = [("dve", dev)]
        pl.wait("tensor", "dve", dev)
        doh, pecnt = emit_gating(
            P, pl, i, lambda c, par=par: xnT_t[par][:, c * 128:(c + 1) * 128],
            wgb, True, gat)
        xnTt_free[par] = [("pe", pecnt)]
    pl.wait("vector", "pe", pl.cnt["pe"])
    dcv = pl.do("vector", lambda h: h.tensor_copy(
        cnt_s[:], P.bankf(3, 512)[0:1, 0:NE]), inc="dve")
    return dcv


def emit_collective(P, pl, myrowT_s, cnt_s, tbl_s, cc_in, cc_out, cnts_s):
    pe = pl.do("tensor", lambda h: h.matmul(
        P.bankf(2, 512)[0:NE, 0:NE], myrowT_s[:], cnt_s[:],
        start=True, stop=True), inc="pe")
    pl.wait("vector", "pe", pe)
    d = pl.do("vector", lambda h: h.tensor_copy(
        tbl_s[:], P.bankf(2, 512)[0:NE, 0:NE]), inc="dve")
    pl.wait("scalar", "dve", d)
    pl.dma("scalar", lambda h: h.dma_start(out=cc_in[:, :], in_=tbl_s[:]), "ccd")
    pl.wait("gpsimd", "ccd")
    pl.cc(lambda h: h.collective_compute(
        "AllReduce", ALU.add, replica_groups=[list(range(8))],
        ins=[cc_in.ap().opt()], outs=[cc_out.ap().opt()]), "ccc")
    pl.wait("gpsimd", "ccc")
    pl.dma("gpsimd", lambda h: h.dma_start(out=cnts_s[:], in_=cc_out[:, :]), "ccr")
    return d


def stage_weights(Wq, Wk, Wv, Wo, wg, W, b):
    def qk(WT):
        return np.ascontiguousarray(
            WT.reshape(HC, 128, 8, 128).transpose(1, 2, 0, 3).reshape(128, 8192))

    def vo(WT):
        return np.ascontiguousarray(
            WT.reshape(HC, 128, H).transpose(1, 0, 2).reshape(128, 8192))

    out = {}
    out["wq"] = qk(np.ascontiguousarray(Wq.T))
    out["wk"] = qk(np.ascontiguousarray(Wk.T))
    out["wv"] = vo(np.ascontiguousarray(Wv.T))
    out["wo"] = vo(np.ascontiguousarray(Wo.T))
    out["wg"] = np.ascontiguousarray(
        wg.reshape(HC, 128, NE).transpose(1, 0, 2).reshape(128, HC * NE))
    WT = W.transpose(0, 2, 1)
    out["wt"] = np.ascontiguousarray(
        WT.reshape(NE, HC, 128, H).transpose(2, 0, 1, 3).reshape(128, NE * HC * H))
    out["b"] = np.ascontiguousarray(b)
    return out


def emit_attention(P, pl, bufs, qsrc, ksrc, vsrc, xres_hbm, dst_hbm, scale,
                   oproj, identb_s, trail=None):
    xbuf = bufs["xbuf"]
    outb = bufs["outb"]
    Abuf, ATb = bufs["Abuf"], bufs["ATb"]
    zp, sm = bufs["zp"], bufs["sm"]
    cbias = bufs["cbias"]
    qload = bufs.get("qload")
    kvm = bufs.get("kv_marks")

    xbuf_free = [[("dve", pl.cnt["dve"])]] * 2
    outb_free = [[("so0", 0)], [("so1", 0)]]
    qbuf_free = [[("pe", pl.cnt["pe"])]] * 2
    bank_free = {g: [("dve", pl.cnt["dve"]), ("act", pl.cnt["act"])]
                 for g in range(8)}
    st8 = {}

    T = trail
    if T is not None:
        tr_xnfree = [[("xt", 0)], [("xt", 0)]]
        tr_prev = [pl.cnt["act"], pl.cnt["act"]]
        tr_mark = {}
        tr_cmm = {}
        tr_doh = {}
        tr_last = {"B": None}
        pl.wait("tensor", "wgl")

    def trail_A(t2, ppar2, dxo2):
        par = t2 % 2
        xn_evt, xfree, pa = emit_ln(P, pl, outb[ppar2][:], T["xn"][par][:],
                                    T["sq"][:], T["stats"][par],
                                    [("dve", dxo2)] + tr_xnfree[par],
                                    tr_prev[par])
        tr_prev[par] = pa
        outb_free[ppar2].extend(xfree)
        war = [("pe", tr_cmm[t2 - 2])] if t2 - 2 in tr_cmm else []
        mk = emit_tr8_dma(P, pl, "sync",
                          lambda c, par=par: T["xn"][par][:, c * 128:(c + 1) * 128],
                          lambda c, par=par: T["xnT"][par][:, c * 128:(c + 1) * 128],
                          "xt", [xn_evt] + war)
        tr_mark[t2] = mk
        tr_xnfree[par] = [("xt", mk)]

    def trail_B(tB, at_ev0):
        pl.waitv("tensor", "xt", tr_mark[tB])
        pl.wait("tensor", "dve",
                at_ev0 if at_ev0 is not None else pl.cnt["dve"])
        if tr_last["B"] is not None:
            pl.wait("tensor", "dve", tr_doh[tr_last["B"]])
        par = tB % 2
        cm = None
        for c in range(HC):
            cm = pl.do("tensor", lambda h, c=c, par=par: h.matmul(
                P.bankf(0, 512)[:, 0:NE],
                T["xnT"][par][:, c * 128:(c + 1) * 128],
                T["wgb"][:, c * NE:(c + 1) * NE],
                start=(c == 0), stop=(c == HC - 1)),
                inc="pe" if c == HC - 1 else None)
        tr_cmm[tB] = cm
        pl.wait("vector", "pe", cm)
        pl.do("vector", lambda h: h.tensor_copy(
            T["lgs"][:], P.bankf(0, 512)[:, 0:NE]))
        d1 = pl.do("vector", lambda h: h.reduce_max(
            T["tmx"][:], T["lgs"][:], axis=AX.X), inc="dve")
        pl.wait("vector", "dve", d1)
        pl.do("vector", lambda h: h.tensor_scalar(
            T["ohb"][0][:], T["lgs"][:], T["tmx"][:, 0:1], 0.0,
            op0=ALU.subtract, op1=ALU.is_ge))
        if tB == 0:
            doh = pl.do("vector", lambda h: h.tensor_copy(
                T["ohsum"][:], T["ohb"][0][:]), inc="dve")
        else:
            doh = pl.do("vector", lambda h: h.tensor_add(
                T["ohsum"][:], T["ohsum"][:], T["ohb"][0][:]), inc="dve")
        tr_doh[tB] = doh
        tr_last["B"] = tB
        return [("dve", doh)]

    def trail_fin():
        pl.wait("tensor", "dve", tr_doh[NT - 1])
        pc = pl.do("tensor", lambda h: h.matmul(
            P.bankf(0, 512)[0:1, 16:16 + NE], T["onescol"][:], T["ohsum"][:],
            start=True, stop=True), inc="pe")
        pl.wait("vector", "pe", pc)
        pl.do("vector", lambda h: h.tensor_copy(
            T["cnt_s"][:], P.bankf(0, 512)[0:1, 16:16 + NE]), inc="dve")

    for j in range(NT + 1):
        par = j % 2
        ppar = (j - 1) % 2
        t = st8.get(j - 1)
        if j < NT:
            if qload is not None:
                pl.wait_many("sync", qbuf_free[par])
                qstrm = qload(pl, j, par)
            else:
                qstrm = None
            pl.wait_many("sync", xbuf_free[par])
            pl.dma("sync", lambda h, j=j, par=par: h.dma_start(
                out=xbuf[par][:], in_=xres_hbm[j * 128:(j + 1) * 128, :]),
                "x%d" % par)
        if t is not None:
            pl.wait("vector", "act", t["exp"][7])
            pl.do("vector", lambda h: h.reduce_sum(sm["zs"][:], zp[:], axis=AX.X))
            drz = pl.do("vector", lambda h: h.reciprocal(sm["rz"][:], sm["zs"][:]),
                        inc="dve")
            pe_tr = {}
            at_ev = {}
            pe_cx = {}
            for bk in (4, 5):
                pl.wait_many("tensor", bank_free[bk])
            pl.wait("tensor", "act", t["exp"][5])
            if kvm is not None:
                pl.wait("tensor", "pkv")

            def emit_tr(tb):
                for kc in range(tb * 8, tb * 8 + 8):
                    pl.wait("tensor", "act", t["exp"][kc // 4])
                    pl.wait("tensor", "act", t["exp"][tb])
                    pe_tr[tb] = pl.do("tensor", lambda h, kc=kc, tb=tb: h.transpose(
                        P.bankb(tb)[:, (kc % 8) * 128:(kc % 8 + 1) * 128],
                        Abuf[:, kc * 128:(kc + 1) * 128], identb_s[:]),
                        inc="pe" if kc % 8 == 7 else None)
                pl.wait("vector", "pe", pe_tr[tb])
                at_ev[tb] = pl.do("vector", lambda h, tb=tb: h.tensor_copy(
                    ATb[:, tb * 1024:(tb + 1) * 1024], P.bankb(tb)), inc="dve")
                bank_free[tb] = [("dve", at_ev[tb])]

            emit_tr(0)
            for tb in range(4):
                if tb + 1 < 4:
                    emit_tr(tb + 1)
                pl.wait("tensor", "dve", at_ev[tb])
                for kc in range(tb * 8, tb * 8 + 8):
                    for half in range(2):
                        pe_cx[half] = pl.do("tensor", lambda h, kc=kc, half=half:
                                            h.matmul(
                            P.bankf(4 + half), ATb[:, kc * 128:(kc + 1) * 128],
                            vsrc(kc, half),
                            start=(kc == 0), stop=(kc == NT - 1)),
                            inc="pe" if kc == NT - 1 else None)
                if tb == 0 and T is not None and j - 3 >= 0:
                    bank_free[0] = bank_free[0] + trail_B(j - 3, at_ev[0])
            if oproj:
                ctxb, ctxT, wob = bufs["ctxb"], bufs["ctxT"], bufs["wob"]
                pl.wait("scalar", "dve", drz)
                cx_ev = None
                for half in range(2):
                    pl.wait("scalar", "pe", pe_cx[half])
                    cx_ev = pl.do("scalar", lambda h, half=half: h.activation(
                        ctxb[:, half * 512:(half + 1) * 512], P.bankf(4 + half),
                        AF.Identity, scale=sm["rz"][:, 0:1]), inc="act")
                    bank_free[4 + half] = [("act", cx_ev)]
                pe_ct = emit_transpose8(
                    P, pl, lambda c: ctxb[:, c * 128:(c + 1) * 128],
                    6, ("act", cx_ev), bank_free[6], identb_s)
                pl.wait("scalar", "pe", pe_ct)
                ct_ev = pl.do("scalar", lambda h: h.activation(
                    ctxT[:], P.bankb(6), AF.Identity), inc="act")
                bank_free[6] = [("act", ct_ev)]
            else:
                pl.wait_many("vector", [("x%d" % ppar, 0)])
                pl.wait_many("vector", outb_free[ppar])
                dxo = None
                for half in range(2):
                    pl.wait("vector", "pe", pe_cx[half])
                    dxo = pl.do("vector", lambda h, half=half, ppar=ppar:
                                h.scalar_tensor_tensor(
                        outb[ppar][:, half * 512:(half + 1) * 512],
                        P.bankf(4 + half), sm["rz"][:, 0:1],
                        xbuf[ppar][:, half * 512:(half + 1) * 512],
                        op0=ALU.mult, op1=ALU.add), inc="dve")
                    bank_free[4 + half] = [("dve", dxo)]
                xbuf_free[ppar] = [("dve", dxo)]
                pl.wait("gpsimd", "dve", dxo)
                gst = pl.dma("gpsimd", lambda h, j=j, ppar=ppar: h.dma_start(
                    out=dst_hbm[(j - 1) * 128:j * 128, :], in_=outb[ppar][:]),
                    "so%d" % ppar)
                outb_free[ppar] = [("so%d" % ppar, gst)]
                t["dxo"] = dxo
                t["ppar"] = ppar
        if j < NT:
            if qstrm is not None:
                pl.wait("tensor", qstrm)
            pe_s = {}
            for gb in range(6):
                pl.wait_many("tensor", bank_free[gb])
                if kvm is not None:
                    pl.waitv("tensor", "pk", kvm[gb])
                for c in range(HC):
                    pe_s[gb] = pl.do("tensor", lambda h, c=c, gb=gb, j=j, par=par:
                                     h.matmul(
                        P.bankf(gb), qsrc(c, j, par), ksrc(c, gb),
                        start=(c == 0), stop=(c == HC - 1)),
                        inc="pe" if c == HC - 1 else None)
        if t is not None and oproj:
            pl.wait("tensor", "act", ct_ev)
            pl.wait_many("tensor", bank_free[7])
            pe_o = {}
            for c in range(HC):
                for half in range(2):
                    pe_o[half] = pl.do("tensor", lambda h, c=c, half=half: h.matmul(
                        P.bankf(6 + half), bufs["ctxT"][:, c * 128:(c + 1) * 128],
                        bufs["wob"][:, c * 1024 + half * 512:
                                    c * 1024 + (half + 1) * 512],
                        start=(c == 0), stop=(c == HC - 1)),
                        inc="pe" if c == HC - 1 else None)
        if t is not None and oproj:
            pl.wait_many("vector", [("x%d" % ppar, 0)])
            pl.wait_many("vector", outb_free[ppar])
            dxo = None
            for half in range(2):
                pl.wait("vector", "pe", pe_o[half])
                dxo = pl.do("vector", lambda h, half=half, ppar=ppar:
                            h.tensor_add(
                    outb[ppar][:, half * 512:(half + 1) * 512],
                    P.bankf(6 + half),
                    xbuf[ppar][:, half * 512:(half + 1) * 512]), inc="dve")
                bank_free[6 + half] = [("dve", dxo)]
            xbuf_free[ppar] = [("dve", dxo)]
            pl.wait("gpsimd", "dve", dxo)
            gst = pl.dma("gpsimd", lambda h, j=j, ppar=ppar: h.dma_start(
                out=dst_hbm[(j - 1) * 128:j * 128, :], in_=outb[ppar][:]),
                "so%d" % ppar)
            outb_free[ppar] = [("so%d" % ppar, gst)]
            t["dxo"] = dxo
            t["ppar"] = ppar
        if j < NT:
            for gb in (6, 7):
                pl.wait_many("tensor", bank_free[gb])
                if kvm is not None:
                    pl.waitv("tensor", "pk", kvm[gb])
                for c in range(HC):
                    pe_s[gb] = pl.do("tensor", lambda h, c=c, gb=gb, j=j, par=par:
                                     h.matmul(
                        P.bankf(gb), qsrc(c, j, par), ksrc(c, gb),
                        start=(c == 0), stop=(c == HC - 1)),
                        inc="pe" if c == HC - 1 else None)
            qbuf_free[par] = [("pe", pe_s[7])]
            if t is not None:
                pl.wait("scalar", "pe", pe_tr[3])
            a_exp = {}
            for gb in range(8):
                pl.wait("scalar", "pe", pe_s[gb])
                a_exp[gb] = pl.do("scalar", lambda h, gb=gb: h.activation(
                    Abuf[:, gb * 512:(gb + 1) * 512], P.bankf(gb), AF.Exp,
                    bias=cbias[:, 0:1], scale=scale,
                    accum_out=zp[:, gb:gb + 1]), inc="act")
            st8[j] = {"exp": a_exp}
        if T is not None and t is not None:
            trail_A(j - 1, t["ppar"], t["dxo"])

    if T is not None:
        trail_B(NT - 2, None)
        trail_B(NT - 1, None)
        trail_fin()


def emit_moe(P, pl, sh, mg, src_hbm, dst_hbm, wtb, stage=None):
    xbuf, sq, xn, stats = sh["xbuf"], sh["sq"], sh["xn"], sh["stats"]
    xnT_t, wgb, sel_s, cnts_s = sh["xnT_t"], sh["wgb"], sh["sel_s"], sh["cnts_s"]
    tri_s, identf_s = sh["tri_s"], sh["identf_s"]
    onesrow_s, onescol_s, identb_s = sh["onesrow_s"], sh["onescol_s"], sh["identb_s"]
    gat = dict(sh["gat"])
    gat.update(mg["gat_x"])
    acc, xob, gt_s, runbase, b_s = mg["acc"], mg["xob"], mg["gt_s"], \
        mg["runbase"], mg["b_s"]
    lg = P.bankf(1, 512)[:, 0:NE]

    xb4 = xbuf + mg["xb2"]
    xn4 = xn + mg["xn2"]
    st4 = stats + mg["st2"]
    xnT4 = xnT_t + mg["xnT2"]
    bank_free = {g: [("dve", pl.cnt["dve"])] for g in range(8)}
    xbuf_free = [[("dve", pl.cnt["dve"])]] * 4
    xn_free = [[("pe", pl.cnt["pe"])]] * 4
    xnTt_free = [[("pe", pl.cnt["pe"])]] * 4
    acc_free = [[("dve", pl.cnt["dve"])]] * 2
    xob_free = [[("so0", 0)], [("so1", 0)]]
    prev_act = [pl.cnt["act"]] * 4
    rb = {"pe": 0, "dve": 0}
    st_stage = {"prev": [pl.cnt["act"]] * 2,
                "xnfree": [[("pe", 0)], [("pe", 0)]],
                "tfree": [[("pe", 0)], [("pe", 0)]]}
    st_prep = {}
    st_ln = {}
    st_dev = {}
    st_gate = {}
    abias_of = {}

    def prep_ln(j):
        par = j % 4
        ldq = "sync" if par % 2 == 0 else "scalar"
        pl.wait_many(ldq, xbuf_free[par])
        sld = pl.dma(ldq, lambda h, j=j, par=par: h.dma_start(
            out=xb4[par][:], in_=src_hbm[j * 128:(j + 1) * 128, :]), "x%d" % par)
        xn_evt, xfree, pa = emit_ln(P, pl, xb4[par][:], xn4[par][:], sq[:],
                                    st4[par], [("x%d" % par, sld)] + xn_free[par],
                                    prev_act[par])
        prev_act[par] = pa
        xbuf_free[par] = list(xfree)
        st_ln[j] = xn_evt

    def prep_tr(j):
        par = j % 4
        pe_t = emit_transpose8(P, pl, lambda c, par=par: xn4[par][:, c * 128:(c + 1) * 128],
                               0, st_ln[j], bank_free[0] + xnTt_free[par],
                               identb_s)
        xn_free[par] = [("pe", pe_t)]
        pl.wait("vector", "pe", pe_t)
        dev = pl.do("vector", lambda h, par=par: h.tensor_copy(
            xnT4[par][:], P.bankb(0)), inc="dve")
        bank_free[0] = [("dve", dev)]
        st_dev[j] = dev

    def prep_lg(j):
        par = j % 4
        pl.wait("tensor", "dve", max(st_dev[j], pl.g_lastoh))
        pl.wait("tensor", "act", pl.g_laste8)
        pe_lg = None
        for c in range(HC):
            pe_lg = pl.do("tensor", lambda h, c=c, par=par: h.matmul(
                lg, xnT4[par][:, c * 128:(c + 1) * 128], wgb[:, c * NE:(c + 1) * NE],
                start=(c == 0), stop=(c == HC - 1)),
                inc="pe" if c == HC - 1 else None)
        st_prep[j] = pe_lg

    def gate(i):
        par = i % 2
        G = gat["G"][par]
        pl.wait("vector", "pe", st_prep[i])
        d = pl.do("vector", lambda h: h.reduce_max(gat["mx8"][:], lg, axis=AX.X),
                  inc="dve")
        pl.wait("vector", "dve", d)
        doh = pl.do("vector", lambda h: h.tensor_scalar(
            gat["oh"][:], lg, gat["mx8"][:, 0:1], 0.0, op0=ALU.subtract,
            op1=ALU.is_ge), inc="dve")
        pl.g_lastoh = doh
        dn = pl.do("vector", lambda h: h.tensor_scalar(
            gat["nmx"][:], gat["mx8"][:], -1.0, 0.0, op0=ALU.mult, op1=ALU.add),
            inc="dve")
        pl.wait("scalar", "dve", dn)
        a8 = pl.do("scalar", lambda h: h.activation(
            gat["e8"][:], lg, AF.Exp, bias=gat["nmx"][:, 0:1], scale=1.0,
            accum_out=gat["z8"][:]), inc="act")
        pl.g_laste8 = a8
        pl.wait("vector", "act", a8)
        pl.do("vector", lambda h: h.reciprocal(gat["gate0"][:], gat["z8"][:]))
        pl.wait_many("tensor", bank_free[2])
        pl.wait("tensor", "dve", max(doh, rb["dve"]))
        pl.do("tensor", lambda h: h.matmul(
            P.bankf(2, 512)[:, 0:NE], tri_s[:], gat["oh"][:], start=True, stop=False))
        peL = pl.do("tensor", lambda h: h.matmul(
            P.bankf(2, 512)[:, 0:NE], onesrow_s[:], runbase[:], start=False,
            stop=True), inc="pe")
        rb["pe"] = peL
        pl.wait_many("tensor", bank_free[3])
        pecs = pl.do("tensor", lambda h: h.matmul(
            P.bankf(3, 512)[0:1, 0:NE], onescol_s[:], gat["oh"][:],
            start=True, stop=True), inc="pe")
        pl.wait("vector", "pe", peL)
        pl.do("vector", lambda h: h.tensor_mul(
            gat["tmp8"][:], P.bankf(2, 512)[:, 0:NE], gat["oh"][:]))
        dls = pl.do("vector", lambda h: h.reduce_sum(
            gat["Ls"][:], gat["tmp8"][:], axis=AX.X), inc="dve")
        pl.do("vector", lambda h: h.tensor_scalar(
            gat["keep"][:], gat["Ls"][:], CAP, 1.0, op0=ALU.is_lt, op1=ALU.mult))
        dgg = pl.do("vector", lambda h: h.tensor_mul(
            gat["gg"][:], gat["keep"][:], gat["gate0"][:]), inc="dve")
        pl.wait("vector", "dve", dgg)
        dG = pl.do("vector", lambda h, G=G: h.tensor_scalar(
            G[:], gat["oh"][:], gat["gg"][:, 0:1], 0.0, op0=ALU.mult,
            op1=ALU.add), inc="dve")
        pl.wait("vector", "pe", max(pecs, rb["pe"]))
        rb["dve"] = pl.do("vector", lambda h: h.tensor_add(
            runbase[:], runbase[:], P.bankf(3, 512)[0:1, 0:NE]), inc="dve")
        st_gate[i] = (dG, dls, rb["dve"], par, G)

    def gate_b(i):
        dG, dls, rbdve, par, G = st_gate[i]
        pl.wait("tensor", "dve", max(dG, rbdve, dls))
        pegt = pl.do("tensor", lambda h, G=G: h.transpose(
            P.ps[0:8, 3 * 512:3 * 512 + 128], G[:], identf_s[:]), inc="pe")
        pl.wait("vector", "pe", pegt)
        dgt = pl.do("vector", lambda h: h.tensor_copy(
            gt_s[:], P.ps[0:8, 3 * 512:3 * 512 + 128]), inc="dve")
        pl.wait("tensor", "dve", dgt)
        pebias = {}
        for half in range(2):
            pebias[half] = pl.do("tensor", lambda h, half=half: h.matmul(
                P.bankf(2 + half), gt_s[:], b_s[0:NE, half * 512:(half + 1) * 512],
                start=True, stop=True), inc="pe")
        pl.wait_many("scalar", acc_free[par])
        abias = None
        for half in range(2):
            pl.wait("scalar", "pe", pebias[half])
            abias = pl.do("scalar", lambda h, half=half, par=par: h.activation(
                acc[par][:, half * 512:(half + 1) * 512], P.bankf(2 + half),
                AF.Identity), inc="act")
        bank_free[2] = [("act", abias), ("dve", dls)]
        bank_free[3] = [("act", abias)]
        abias_of[i] = abias

    prep_ln(0)
    prep_tr(0)
    prep_lg(0)
    prep_ln(1)
    prep_tr(1)
    pl.wait("tensor", "ccr")
    pl.wait("tensor", "c")
    pl.wait("tensor", "dve", pl.cnt["dve"])
    perb = pl.do("tensor", lambda h: h.matmul(
        P.bankf(2, 512)[0:1, 0:NE], sel_s[:], cnts_s[:], start=True, stop=True),
        inc="pe")
    pl.wait("vector", "pe", perb)
    drb = pl.do("vector", lambda h: h.tensor_copy(
        runbase[:], P.bankf(2, 512)[0:1, 0:NE]), inc="dve")
    rb["pe"] = perb
    rb["dve"] = drb
    gate(0)
    gate_b(0)
    prep_lg(1)
    for i in range(NT):
        par = i % 2
        par4 = i % 4
        ebanks = [(4, 5), (6, 7)]
        pe_e = {}
        dacc = None
        abias = abias_of.pop(i)
        for e in range(NE):
            ba, bb = ebanks[e % 2]
            pl.wait("tensor", "wt%d" % e)
            for bk in (ba, bb):
                pl.wait_many("tensor", bank_free[bk])
            for c in range(HC):
                for half, bk in ((0, ba), (1, bb)):
                    pe_e[(e, half)] = pl.do(
                        "tensor", lambda h, c=c, e=e, bk=bk, half=half, par4=par4:
                        h.matmul(
                            P.bankf(bk), xnT4[par4][:, c * 128:(c + 1) * 128],
                            wtb[:, (e * 8 + c) * 1024 + half * 512:
                                (e * 8 + c) * 1024 + (half + 1) * 512],
                            start=(c == 0), stop=(c == HC - 1)),
                        inc="pe" if c == HC - 1 else None)
            if e == 1 and i + 2 < NT:
                prep_ln(i + 2)
            if e == 3 and i + 1 < NT:
                if i + 2 < NT:
                    prep_tr(i + 2)
            if e == 4 and i + 1 < NT:
                gate(i + 1)
            if e == 5 and i + 2 < NT:
                prep_lg(i + 2)
            if e == 6 and i + 1 < NT:
                gate_b(i + 1)
            if e == 0:
                pl.wait("vector", "act", abias)
            for half, bk in ((0, ba), (1, bb)):
                pl.wait("vector", "pe", pe_e[(e, half)])
                dacc = pl.do("vector", lambda h, half=half, bk=bk, e=e, par=par:
                             h.scalar_tensor_tensor(
                    acc[par][:, half * 512:(half + 1) * 512], P.bankf(bk),
                    gat["G"][par][:, e:e + 1],
                    acc[par][:, half * 512:(half + 1) * 512],
                    op0=ALU.mult, op1=ALU.add), inc="dve")
                bank_free[bk] = [("dve", dacc)]
        xnTt_free[par4] = [("pe", pl.cnt["pe"])]
        pl.wait_many("vector", xob_free[par])
        dxo = pl.do("vector", lambda h, par=par, par4=par4: h.tensor_add(
            xob[par][:], acc[par][:], xb4[par4][:]), inc="dve")
        acc_free[par] = [("dve", dxo)]
        xbuf_free[par4] = xbuf_free[par4] + [("dve", dxo)]
        pl.wait("gpsimd", "dve", dxo)
        gst = pl.dma("gpsimd", lambda h, i=i, par=par: h.dma_start(
            out=dst_hbm[i * 128:(i + 1) * 128, :], in_=xob[par][:]), "so%d" % par)
        xob_free[par] = [("so%d" % par, gst)]


def alloc_moe_group(P, wt_name):
    mg = {}
    mg["wtb"] = P.sb(wt_name, [128, NE * HC * H], BF16)
    mg["acc"] = [P.sb(f"{wt_name}_acc{p}", [128, H], F32) for p in range(2)]
    mg["xob"] = [P.sb(f"{wt_name}_xob{p}", [128, H], F32) for p in range(2)]
    mg["b_s"] = P.sb(f"{wt_name}_b_s", [NE, H], F32)
    mg["gt_s"] = P.sb(f"{wt_name}_gt", [NE, 128], F32)
    mg["runbase"] = P.sb(f"{wt_name}_rb", [1, NE], F32)
    mg["xb2"] = [P.sb(f"{wt_name}_xb{p}", [128, H], F32) for p in range(2)]
    mg["xn2"] = [P.sb(f"{wt_name}_xn{p}", [128, H], BF16) for p in range(2)]
    mg["xnT2"] = [P.sb(f"{wt_name}_xnT{p}", [128, H], BF16) for p in range(2)]
    mg["st2"] = [{k: P.sb(f"{wt_name}_s{k}{p}", [128, 1], F32)
                  for k in ("mraw", "mean", "msq", "var", "lnv", "rstd",
                            "sums", "nb2")} for p in range(2)]
    mg["sxn"] = [P.sb(f"{wt_name}_sxn{p}", [128, H], BF16) for p in range(2)]
    mg["sxnT"] = [P.sb(f"{wt_name}_sxT{p}", [128, H], BF16) for p in range(2)]
    mg["sst"] = [{k: P.sb(f"{wt_name}_t{k}{p}", [128, 1], F32)
                  for k in ("mraw", "mean", "msq", "var", "lnv", "rstd",
                            "sums", "nb2")} for p in range(2)]
    gx = {}
    for k, s in (("tmp8", NE), ("Ls", 1), ("keep", 1), ("gg", 1)):
        gx[k] = P.sb(f"{wt_name}_g{k}", [128, s], F32)
    gx["G"] = [P.sb(f"{wt_name}_gG{p}", [128, NE], F32) for p in range(2)]
    mg["gat_x"] = gx
    return mg




def build_fused():
    P = Prog()
    nc = P.nc
    x_in = nc.dram_tensor("x", [SEQ, H], F32, kind="ExternalInput").ap()
    wq_in = nc.dram_tensor("wq", [128, 8192], F32, kind="ExternalInput").ap()
    wk_in = nc.dram_tensor("wk", [128, 8192], F32, kind="ExternalInput").ap()
    wv_in = nc.dram_tensor("wv", [128, 8192], F32, kind="ExternalInput").ap()
    wo_in = nc.dram_tensor("wo", [128, 8192], F32, kind="ExternalInput").ap()
    wg_in = nc.dram_tensor("wg", [128, HC * NE], F32, kind="ExternalInput").ap()
    wt_in = nc.dram_tensor("wt", [128, NE * HC * H], F32, kind="ExternalInput").ap()
    b_in = nc.dram_tensor("b", [NE, H], F32, kind="ExternalInput").ap()
    sel_in = nc.dram_tensor("sel", [NE, 1], F32, kind="ExternalInput").ap()
    myrow_in = nc.dram_tensor("myrow", [1, NE], F32, kind="ExternalInput").ap()
    y_out = nc.dram_tensor("y", [SEQ, H], F32, kind="ExternalOutput").ap()

    qt_hbm = nc.dram_tensor("qt_hbm", [NT, 128, 8, 128], BF16, kind="Internal").ap()
    xn2s_hbm = nc.dram_tensor("xn2s_hbm", [128, NT * H], BF16, kind="Internal").ap()
    xnT2s_hbm = nc.dram_tensor("xnT2s_hbm", [128, HC * SEQ], BF16,
                               kind="Internal").ap()
    x1_hbm = nc.dram_tensor("x1_hbm", [SEQ, H], F32, kind="Internal").ap()
    x2_hbm = nc.dram_tensor("x2_hbm", [SEQ, H], F32, kind="Internal").ap()
    x3_hbm = nc.dram_tensor("x3_hbm", [SEQ, H], F32, kind="Internal").ap()
    cc1i = nc.dram_tensor("cc1i", [NE, NE], F32)
    cc1o = nc.dram_tensor("cc1o", [NE, NE], F32)
    cc2i = nc.dram_tensor("cc2i", [NE, NE], F32)
    cc2o = nc.dram_tensor("cc2o", [NE, NE], F32)

    pl = Pl(nc)

    xbuf = [P.sb(f"xbuf{p}", [128, H], F32) for p in range(2)]
    sq = P.sb("sq", [128, H], BF16)
    xn = [P.sb(f"xn{p}", [128, H], BF16) for p in range(2)]
    stats = [{k: P.sb(f"st_{k}{p}", [128, 1], F32)
              for k in ("mraw", "mean", "msq", "var", "lnv", "rstd", "sums", "nb2")}
             for p in range(2)]
    onescol_s = P.sb("onescol", [128, 1], F32)
    onesrow_s = P.sb("onesrow", [1, 128], F32)
    identb_s = P.sb("identb_s", [128, 128], BF16)
    identf_s = P.sb("identf_s", [128, 128], F32)
    tri_s = P.sb("tri_s", [128, 128], F32)
    wgb = P.sb("wgb", [128, HC * NE], BF16)
    sel_s = P.sb("sel_s", [NE, 1], F32)
    myrowT_s = P.sb("myrowT_s", [1, NE], F32)
    cnts_s = P.sb("cnts_s", [NE, NE], F32)
    cnt_s = P.sb("cnt_s", [1, NE], F32)
    tbl_s = P.sb("tbl_s", [NE, NE], F32)
    xnT_t = [P.sb(f"xnTt{p}", [128, H], BF16) for p in range(2)]
    gat = {}
    for k, s in (("mx8", 1), ("oh", NE), ("nmx", 1), ("e8", NE), ("z8", 1),
                 ("gate0", 1)):
        gat[k] = P.sb("g_" + k, [128, s], F32)
    gat["onescol"] = onescol_s
    sh = {"xbuf": xbuf, "sq": sq, "xn": xn, "stats": stats, "xnT_t": xnT_t,
          "wgb": wgb, "sel_s": sel_s, "cnts_s": cnts_s, "tri_s": tri_s,
          "identf_s": identf_s, "onesrow_s": onesrow_s, "onescol_s": onescol_s,
          "identb_s": identb_s, "gat": gat}
    trail = {"xn": xn, "sq": sq, "stats": stats, "xnT": xnT_t, "wgb": wgb,
             "onescol": onescol_s, "cnt_s": cnt_s,
             "lgs": P.sb("tr_lgs", [128, NE], F32),
             "tmx": P.sb("tr_tmx", [128, 1], F32),
             "ohsum": P.sb("tr_os", [128, NE], F32),
             "ohb": [P.sb(f"tr_oh{p}", [128, NE], F32) for p in range(2)]}

    for buf, src in ((identb_s, P.identb.ap()), (identf_s, P.identf.ap()),
                     (tri_s, P.c_triu.ap()), (onescol_s, P.c_onescol.ap()),
                     (onesrow_s, P.c_onesrow.ap()), (sel_s, sel_in),
                     (myrowT_s, myrow_in)):
        pl.dma("sync", lambda h, buf=buf, src=src: h.dma_start(out=buf[:], in_=src),
               "c")
    pl.dma("gpsimd", lambda h: h.dma_start(out=wgb[:], in_=wg_in), "wgl")
    pl.wait("tensor", "c")

    markA = len(P.guards)
    KT = P.sb("KT", [128, HC * SEQ], BF16)
    V = P.sb("Vb", [128, NT * H], BF16)
    markA1 = len(P.guards)
    wvb = P.sb("wvb", [128, 8192], BF16)
    wring = [P.sb(f"wring{p}", [128, 1024], BF16) for p in range(4)]
    xnTg = [P.sb(f"xnTg{p}", [128, HC * 512], BF16) for p in range(2)]
    qstage = P.sb("qstage", [128, 8 * 512], BF16)

    pl.dma("gpsimd", lambda h: h.dma_start(out=wvb[:], in_=wv_in), "wv")

    bank_free = {g: [("dve", 0)] for g in range(8)}
    xbuf_free = [[("dve", 0)]] * 2
    xn_free = [[("pe", 0)]] * 2
    xnTg_free = [("pe", 0), ("pe", 0)]
    qstage_free = ("qs", 0)
    ring_free = [("pe", 0)] * 4
    prev_act = [0, 0]

    for i in range(NT):
        par = i % 2
        g, m = i // 4, i % 4
        pl.wait_many("sync", xbuf_free[par])
        sld = pl.dma("sync", lambda h, i=i, par=par: h.dma_start(
            out=xbuf[par][:], in_=x_in[i * 128:(i + 1) * 128, :]), "x%d" % par)
        xn_evt, xfree, pa = emit_ln(P, pl, xbuf[par][:], xn[par][:], sq[:], stats[par],
                                    [("x%d" % par, sld)] + xn_free[par], prev_act[par])
        prev_act[par] = pa
        xbuf_free[par] = list(xfree)
        gpar = g % 2
        extra = [xnTg_free[gpar]] if m == 0 else []
        pe_t = emit_transpose8(P, pl, lambda c, par=par: xn[par][:, c * 128:(c + 1) * 128],
                               6, xn_evt, bank_free[6] + extra, identb_s)
        xn_free[par] = [("pe", pe_t)]
        pl.wait("vector", "pe", pe_t)
        dst = xnTg[gpar][:].rearrange("p (c t) -> p c t", c=HC)[:, :, m * 128:(m + 1) * 128]
        src = P.bankb(6)[:].rearrange("p (c q) -> p c q", c=HC)
        dev = pl.do("vector", lambda h, dst=dst, src=src: h.tensor_copy(dst, src),
                    inc="dve")
        bank_free[6] = [("dve", dev)]
        pl.wait("tensor", "dve", dev)
        if i == 0:
            pl.wait("tensor", "wv")
        for bk in (0, 1):
            pl.wait_many("tensor", bank_free[bk])
        pev = {}
        for half in range(2):
            for c in range(HC):
                pev[half] = pl.do("tensor", lambda h, half=half, c=c, m=m, gpar=gpar:
                                  h.matmul(
                    P.bankf(half),
                    xnTg[gpar][:, c * 512 + m * 128: c * 512 + (m + 1) * 128],
                    wvb[:, c * 1024 + half * 512: c * 1024 + (half + 1) * 512],
                    start=(c == 0), stop=(c == HC - 1)),
                    inc="pe" if c == HC - 1 else None)
        for half in range(2):
            pl.wait("vector", "pe", pev[half])
            dv = pl.do("vector", lambda h, half=half, i=i: h.tensor_copy(
                V[:, i * 1024 + half * 512: i * 1024 + (half + 1) * 512],
                P.bankf(half)), inc="dve")
            bank_free[half] = [("dve", dv)]

        if m == 3:
            seq16 = [("q", wq_in, s) for s in range(8)] + \
                    [("k", wk_in, s) for s in range(8)]

            def wload(t):
                which, win, s = seq16[t]
                slot = t % 4
                pl.wait_many("gpsimd", [ring_free[slot]])
                pl.dma("gpsimd", lambda h, win=win, s=s, slot=slot: h.dma_start(
                    out=wring[slot][:], in_=win[:, s * 1024:(s + 1) * 1024]),
                    "wr%d" % slot)

            for t in range(4):
                wload(t)
            for t in range(16):
                which, win, s = seq16[t]
                slot = t % 4
                pl.wait("tensor", "wr%d" % slot)
                pl.wait_many("tensor", bank_free[s])
                if which == "q" and s == 0:
                    pl.wait_many("tensor", [qstage_free])
                pes = None
                for c in range(HC):
                    pes = pl.do("tensor", lambda h, s=s, c=c, slot=slot, gpar=gpar:
                                h.matmul(
                        P.bankf(s), wring[slot][:, c * 128:(c + 1) * 128],
                        xnTg[gpar][:, c * 512:(c + 1) * 512],
                        start=(c == 0), stop=(c == HC - 1)),
                        inc="pe" if c == HC - 1 else None)
                ring_free[slot] = ("pe", pes)
                if t + 4 < 16:
                    wload(t + 4)
                pl.wait("vector", "pe", pes)
                if which == "q":
                    pl.wait_many("vector", [qstage_free])
                    dv = pl.do("vector", lambda h, s=s: h.tensor_copy(
                        qstage[:, s * 512:(s + 1) * 512], P.bankf(s)), inc="dve")
                else:
                    dv = pl.do("vector", lambda h, s=s, g=g: h.tensor_copy(
                        KT[:, s * SEQ + g * 512: s * SEQ + (g + 1) * 512],
                        P.bankf(s)), inc="dve")
                bank_free[s] = [("dve", dv)]
                if which == "q" and s == 7:
                    pl.wait("gpsimd", "dve", pl.cnt["dve"])
                    gpv = None
                    for mm in range(4):
                        i2 = 4 * g + mm
                        srcq = qstage[:].rearrange("p (s q) -> p s q", s=8)[
                            :, :, mm * 128:(mm + 1) * 128]
                        gpv = pl.dma("gpsimd", lambda h, i2=i2, srcq=srcq: h.dma_start(
                            out=qt_hbm[i2], in_=srcq), "qs")
                    qstage_free = ("qs", gpv)
            xnTg_free[gpar] = ("pe", pl.cnt["pe"])

    pl.join()
    P.free_to(markA1)

    wob = P.sb("wob", [128, 8192], BF16)
    qbuf = [P.sb(f"qbuf{p}", [128, 1024], BF16) for p in range(2)]
    bufs = {
        "Abuf": P.sb("Abuf", [128, SEQ], BF16),
        "ATb": P.sb("ATb", [128, SEQ], BF16),
        "ctxb": P.sb("ctxb", [128, H], BF16),
        "ctxT": P.sb("ctxT", [128, H], BF16),
        "cbias": P.sb("cbias", [128, 1], F32),
        "zp": P.sb("zp", [128, 8], F32),
        "sm": {k: P.sb("sm_" + k, [128, 1], F32) for k in ("zs", "rz")},
        "xbuf": xbuf,
        "outb": [P.sb(f"x1b{p}", [128, H], F32) for p in range(2)],
        "wob": wob,
    }
    pl.dma("gpsimd", lambda h: h.dma_start(out=wob[:], in_=wo_in), "wo")
    pl.dma("sync", lambda h: h.dma_start(out=bufs["cbias"][:], in_=P.c_zero1.ap()), "c")
    pl.wait("tensor", "wo")
    pl.wait("scalar", "c")

    def qload(pl, j, par):
        pl.dma("sync", lambda h, j=j, par=par: h.dma_start(
            out=qbuf[par][:], in_=qt_hbm[j].rearrange("p s q -> p (s q)")),
            "q%d" % par)
        return "q%d" % par

    bufs["qload"] = qload
    emit_attention(
        P, pl, bufs,
        qsrc=lambda c, j, par: qbuf[par][:, c * 128:(c + 1) * 128],
        ksrc=lambda c, gb: KT[:, c * SEQ + gb * 512: c * SEQ + (gb + 1) * 512],
        vsrc=lambda kc, half: V[:, kc * 1024 + half * 512: kc * 1024 + (half + 1) * 512],
        xres_hbm=x_in, dst_hbm=x1_hbm, scale=1.0 / 32.0, oproj=True,
        identb_s=identb_s, trail=trail)

    pl.join()
    P.free_to(markA)

    markM = len(P.guards)
    mg1 = alloc_moe_group(P, "wt1")
    for e in range(NE):
        pl.dma("gpsimd", lambda h, e=e: h.dma_start(
            out=mg1["wtb"][:, e * 8192:(e + 1) * 8192],
            in_=wt_in[:, e * 8192:(e + 1) * 8192]), "wt%d" % e)
    pl.dma("sync", lambda h: h.dma_start(out=mg1["b_s"][:], in_=b_in), "c")
    emit_collective(P, pl, myrowT_s, cnt_s, tbl_s, cc1i, cc1o, cnts_s)

    emit_moe(P, pl, sh, mg1, x1_hbm, x2_hbm, mg1["wtb"])
    pl.join()
    P.free_to(markM)

    markB = len(P.guards)
    xn2 = P.sb("xn2", [128, NT * H], BF16)
    xnT2 = P.sb("xnT2", [128, HC * SEQ], BF16)
    bufsB = {
        "Abuf": P.sb("Abuf2", [128, SEQ], BF16),
        "ATb": P.sb("ATb2", [128, SEQ], BF16),
        "cbias": P.sb("cbias2", [128, 1], F32),
        "zp": P.sb("zp2", [128, 8], F32),
        "sm": {k: P.sb("sm2_" + k, [128, 1], F32) for k in ("zs", "rz")},
        "xbuf": xbuf,
        "outb": [P.sb(f"x3b{p}", [128, H], F32) for p in range(2)],
    }
    pl.dma("sync", lambda h: h.dma_start(out=bufsB["cbias"][:], in_=P.c_m1024.ap()),
           "c")
    pl.wait("scalar", "c")
    xb4 = xbuf + [P.sb(f"xbufB{p}", [128, H], F32) for p in range(2)]
    st4 = stats + [{k: P.sb(f"stB_{k}{p}", [128, 1], F32)
                    for k in ("mraw", "mean", "msq", "var", "lnv", "rstd",
                              "sums", "nb2")} for p in range(2)]
    prev_act = [pl.cnt["act"]] * 4
    xbuf_free = [[("dve", pl.cnt["dve"])]] * 4
    bk_free = [[("dve", pl.cnt["dve"])]] * 2
    for i in range(NT):
        par = i % 4
        bk = i % 2
        ldq = "sync" if par % 2 == 0 else "gpsimd"
        pl.wait_many(ldq, xbuf_free[par])
        sld = pl.dma(ldq, lambda h, i=i, par=par: h.dma_start(
            out=xb4[par][:], in_=x2_hbm[i * 128:(i + 1) * 128, :]), "x%d" % par)
        xn_evt, xfree, pa = emit_ln(P, pl, xb4[par][:], xn2[:, i * 1024:(i + 1) * 1024],
                                    sq[:], st4[par], [("x%d" % par, sld)],
                                    prev_act[par])
        prev_act[par] = pa
        xbuf_free[par] = list(xfree)
        pe_t = emit_transpose8(
            P, pl,
            lambda c, i=i: xn2[:, i * 1024 + c * 128: i * 1024 + (c + 1) * 128],
            bk, xn_evt, bk_free[bk], identb_s)
        pl.wait("vector", "pe", pe_t)
        dst = xnT2[:].rearrange("p (c t) -> p c t", c=HC)[:, :, i * 128:(i + 1) * 128]
        src = P.bankb(bk)[:].rearrange("p (c q) -> p c q", c=HC)
        dev = pl.do("vector", lambda h, dst=dst, src=src: h.tensor_copy(dst, src),
                    inc="dve")
        bk_free[bk] = [("dve", dev)]
    pl.join()

    emit_attention(
        P, pl, bufsB,
        qsrc=lambda c, j, par: xnT2[:, c * SEQ + j * 128: c * SEQ + (j + 1) * 128],
        ksrc=lambda c, gb: xnT2[:, c * SEQ + gb * 512: c * SEQ + (gb + 1) * 512],
        vsrc=lambda kc, half: xn2[:, kc * 1024 + half * 512: kc * 1024 + (half + 1) * 512],
        xres_hbm=x2_hbm, dst_hbm=x3_hbm, scale=1.0, oproj=False,
        identb_s=identb_s, trail=trail)

    pl.join()
    P.free_to(markB)

    markM2 = len(P.guards)
    mg2 = alloc_moe_group(P, "wt2")
    for e in range(NE):
        pl.dma("gpsimd", lambda h, e=e: h.dma_start(
            out=mg2["wtb"][:, e * 8192:(e + 1) * 8192],
            in_=wt_in[:, e * 8192:(e + 1) * 8192]), "wt%d" % e)
    pl.dma("sync", lambda h: h.dma_start(out=mg2["b_s"][:], in_=b_in), "c")
    emit_collective(P, pl, myrowT_s, cnt_s, tbl_s, cc2i, cc2o, cnts_s)

    emit_moe(P, pl, sh, mg2, x3_hbm, y_out, mg2["wtb"])

    for strm in ("so0", "so1"):
        pl.wait("gpsimd", strm)
        pl.wait("sync", strm)

    block = nc.Block()
    with block:
        pl.play(block)
    return nc



_CACHE = {}


def _get_prog():
    if "p" not in _CACHE:
        _CACHE["p"] = build_fused()
    return _CACHE["p"]


def kernel(L0_attn_a_x, L0_attn_a_W_qkv, L0_attn_a_W_out, L0_ffn_wg, L0_ffn_W,
           L0_ffn_b):
    x = np.ascontiguousarray(np.asarray(L0_attn_a_x, dtype=np.float32))
    Wqkv = np.asarray(L0_attn_a_W_qkv, dtype=np.float32)
    Wq, Wk, Wv = Wqkv[0::3], Wqkv[1::3], Wqkv[2::3]
    st = stage_weights(Wq, Wk, Wv, np.asarray(L0_attn_a_W_out, np.float32),
                       np.asarray(L0_ffn_wg, np.float32),
                       np.asarray(L0_ffn_W, np.float32),
                       np.asarray(L0_ffn_b, np.float32))
    cores = list(range(8))
    prog = _get_prog()

    ins = []
    for c in cores:
        sel = np.ascontiguousarray(
            (np.arange(NE) < c).astype(np.float32).reshape(NE, 1))
        myrow = np.zeros((1, NE), np.float32)
        myrow[0, c] = 1.0
        ins.append({"x": np.ascontiguousarray(x[c]), "wq": st["wq"],
                    "wk": st["wk"], "wv": st["wv"], "wo": st["wo"],
                    "wg": st["wg"], "wt": st["wt"], "b": st["b"],
                    "sel": sel, "myrow": myrow})
    r = run_bass_kernel_spmd(prog, ins, core_ids=cores)
    out = np.stack([r.results[c]["y"] for c in cores]).astype(np.float32)
    return out
